# version 07
# baseline (speedup 1.0000x reference)
import math
from contextlib import ExitStack
from types import SimpleNamespace

import numpy as np
import concourse.bass as bass
import concourse.mybir as mybir
from concourse.bass_utils import run_bass_kernel_spmd

F32 = mybir.dt.float32
BF16 = mybir.dt.bfloat16
I32 = mybir.dt.int32
AF = mybir.ActivationFunctionType
ALU = mybir.AluOpType
AX = mybir.AxisListType
PI = math.pi
TWO_PI = 2.0 * math.pi

D, NCH, TB, TOK = 4096, 32, 512, 1024
NRES = 24
NPARK = NCH - NRES
NH0, NKV0, HD0 = 16, 4, 64
NH1, NKV1, HD1 = 32, 8, 128
EPS_LN, EPS_RMS = 1e-5, 1e-6
OWNL = [0, 512]
SC1 = 1.0 / math.sqrt(128.0)
KT0 = [32, 64]

ENGS = ("sync", "gpsimd", "vector", "scalar", "tensor")
DPOOL = {"sync": 8, "gpsimd": 4, "scalar": 4}


def blk_owner(g):
    r = min(g, 15 - g)
    return r, (0 if g <= 7 else 1)


class Script:

    def __init__(self, S, tag):
        self.S, self.tag = S, tag
        self.prog = {n: [] for n in ENGS}
        self.ncoll = 0
        self.drain = None
        if getattr(S, "pool", None) is None:
            S.pool = {
                "c": {n: [S.sem(f"plc_{n}"), 0] for n in ENGS},
                "d": {n: [[S.sem(f"pld_{n}{k}") for k in range(DPOOL[n])],
                          [0] * DPOOL[n], 0] for n in DPOOL},
                "lastc": {n: None for n in ENGS},
                "colls": [],
            }
        self.pool = S.pool

    def snap_drain(self):
        d = []
        for n in DPOOL:
            dp = self.pool["d"][n]
            for p in range(len(dp[0])):
                if dp[1][p]:
                    d.append((dp[0][p], 16 * dp[1][p]))
        d += [(sem, 1) for sem in self.pool["colls"]]
        self.drain = d

    def step(self, eng, fn, deps=(), dma=False, coll=False, chain=None):
        deps = [t for t in deps if t is not None]
        if coll:
            sem = self.S.sem(f"{self.tag}_cl{self.ncoll}")
            self.pool["colls"].append(sem)
            self.ncoll += 1
            tok = ("s", sem, 1)
            self.prog[eng].append((fn, tuple(deps), tok, None, 1))
            return tok
        if dma:
            dp = self.pool["d"][eng]
            p = dp[2] % len(dp[0])
            dp[2] += 1
            dp[1][p] += 1
            u = dp[1][p]
            tok = ("d", eng, p, u)
            selfwait = ("d", eng, p, u - 1) if u > 1 else None
            self.prog[eng].append((fn, tuple(deps), tok, selfwait, 16))
            return tok
        cp = self.pool["c"][eng]
        cp[1] += 1
        tok = ("c", eng, cp[1])
        if chain is None:
            chain = True
        if chain and self.pool["lastc"][eng] is not None:
            deps = deps + [self.pool["lastc"][eng]]
        self.pool["lastc"][eng] = tok
        self.prog[eng].append((fn, tuple(deps), tok, None, 1))
        return tok

    def _resolve(self, t):
        if t[0] == "c":
            return self.pool["c"][t[1]][0], t[2]
        if t[0] == "d":
            return self.pool["d"][t[1]][0][t[2]], 16 * t[3]
        return t[1], t[2]

    def _emit(self, engname, e):
        S = self.S
        for fn, deps, tok, selfwait, inc in self.prog[engname]:
            fold = None
            waits = {}
            for t in deps:
                sem, val = self._resolve(t)
                if fold is None and t[0] == "c" and t[1] == engname:
                    fold = (sem, val)
                    continue
                k = id(sem)
                if k not in waits or waits[k][1] < val:
                    waits[k] = (sem, val)
            if selfwait is not None:
                sem, val = self._resolve(selfwait)
                k = id(sem)
                if k not in waits or waits[k][1] < val:
                    waits[k] = (sem, val)
            for sem, val in waits.values():
                e.wait_ge(sem, val)
            ins = fn(S, e)
            if ins is None:
                ins = e.nop()
            if fold is not None:
                ins._wait_ge(fold[0], fold[1])
            ins.then_inc(self._resolve(tok)[0], inc)
        if engname == "sync" and self.drain:
            for sem, val in self.drain:
                e.wait_ge(sem, val)


def build_state(core_id=0):
    nc = bass.Bass("TRN2", target_bir_lowering=False)
    S = SimpleNamespace(nc=nc, core=core_id, pool=None)

    di = lambda n, sh, dt=F32: nc.dram_tensor(n, sh, dt, kind="ExternalInput").ap()
    S.x = di("x", [D, TOK])
    S.qkvT = di("qkvT", [12, 128, D])
    S.qc1 = di("qc1", [8, 128, 384])
    S.qc2T = di("qc2T", [8, 128, 3072])
    S.kc1 = di("kc1", [2, 128, 384])
    S.kc2T = di("kc2T", [2, 128, 768])
    S.cb_q1 = di("cb_q1", [128, 8])
    S.cb_q2 = di("cb_q2", [128, 8])
    S.cb_k1 = di("cb_k1", [128, 2])
    S.cb_k2 = di("cb_k2", [128, 2])
    S.key_temp = di("key_temp", [1, 1])
    S.woT = di("woT", [32, 128, 1024])
    S.ffn0T = di("ffn0T", [32, 128, D])
    S.nwc = di("nwc", [128, 32])
    S.gamma = di("gamma", [128, 32])
    S.l1qT = di("l1qT", [32, 128, D])
    S.l1kT = di("l1kT", [8, 128, D])
    S.l1vT = di("l1vT", [8, 128, D])
    S.l1oT = di("l1oT", [32, 128, D])
    S.codes = di("codes", [2, 512 * 512], I32)
    S.cbflat = di("cbflat", [512, 8])
    S.scales = di("scales", [128, 32])
    S.fbias = di("fbias", [128, 32])
    S.pbase = di("pbase", [1, 16])
    S.prm = di("prm", [1, 16], I32)
    S.hz = di("hz", [128, 2])

    S.out = nc.dram_tensor("out", [D, TOK], F32, kind="ExternalOutput").ap()

    dd = lambda n, sh, dt=BF16: nc.dram_tensor(n, sh, dt).ap()
    S.xpark = dd("xpark", [NPARK, 128, TOK], F32)
    S.hqc = dd("hqc", [2, 12, 128, 4])
    S.hqg = dd("hqg", [8, 2, 12, 128, 4])
    S.k0c = dd("k0c", [NKV0, 64, TOK])
    S.v0c = dd("v0c", [TOK, NKV0 * 65])
    S.k0g = dd("k0g", [8, NKV0, 64, TOK])
    S.v0g = dd("v0g", [8, TOK, NKV0 * 65])
    S.qfd = dd("qfd", [8, 128, TOK])
    S.q1d = dd("q1d", [NH1, 128, TOK])
    S.k1c = dd("k1c", [2, NKV1, 128, 512])
    S.v1c = dd("v1c", [2, 512, NKV1 * 129])
    S.v1f = dd("v1f", [8, 128, TOK])
    S.k1g = dd("k1g", [16, NKV1, 128, 512])
    S.v1g = dd("v1g", [16, 512, NKV1 * 129])
    S.k1p = dd("k1p", [2, NKV1, 128, 512])
    S.v1p = dd("v1p", [2, 512, NKV1 * 129])
    S.mx1c = dd("mx1c", [1, 8], F32)
    S.mx1g = dd("mx1g", [8, 8], F32)
    S.ao0 = dd("ao0", [8, 128, TOK])
    S.ao1 = dd("ao1", [NCH, 128, TOK])
    S.wqc = dd("wqc", [D, 512])
    S.codesb = dd("codesb", [2, 512 * 512])
    S.ffn0b = dd("ffn0b", [32, 128, D])
    S.l1qb = dd("l1qb", [32, 128, D])
    S.l1kb = dd("l1kb", [8, 128, D])
    S.l1vb = dd("l1vb", [8, 128, D])
    S.l1ob = dd("l1ob", [32, 128, D])
    S.wqg = dd("wqg", [8, D, 512])
    S.rt = dd("rt", [8, 512], F32)

    ctx = ExitStack()
    S.ctx = ctx
    S.sem = lambda name: ctx.enter_context(nc.semaphore(name))
    S.sb = lambda name, shape, dt=BF16: ctx.enter_context(nc.sbuf_tensor(name, shape, dt))
    S.psum = lambda name, shape, dt=F32: ctx.enter_context(nc.psum_tensor(name, shape, dt))

    S.xs = S.sb("xs", [128, NRES * TOK], F32)
    S.xw = [S.sb(f"xw{k}", [128, TOK], F32) for k in range(2)]
    S.xr = [S.sb(f"xr{k}", [128, 512], F32) for k in range(2)]
    S.xn = S.sb("xn", [128, NCH * 512], BF16)
    S.wb = [S.sb(f"wb{k}", [128, NCH * 128], BF16) for k in range(2)]
    S.arena = S.sb("arena", [128, 12384], BF16)
    S.ones_c = S.sb("ones_c", [128, 1], BF16)
    S.ones_cf = S.sb("ones_cf", [128, 1], F32)
    S.ones_r32 = S.sb("ones_r32", [1, 128], F32)
    S.identb = S.sb("identb", [128, 128], BF16)
    S.identf = S.sb("identf", [128, 128], F32)
    S.ktep = S.sb("ktep", [128, 1], F32)
    S.nktc = S.sb("nktc", [128, 1], F32)
    S.invfr = S.sb("invfr", [128, 128], F32)
    S.signr = S.sb("signr", [128, 128], F32)
    S.posc = S.sb("posc", [128, 8], F32)
    S.pbr = S.sb("pbr", [128, 12], F32)
    S.hzr = S.sb("hzr", [128, 2], F32)
    S.gcols = S.sb("gcols", [128, 32], F32)
    S.nwcols = S.sb("nwcols", [128, 32], F32)
    S.scol = S.sb("scol", [128, 32], F32)
    S.bcol = S.sb("bcol", [128, 32], F32)
    S.cbq1 = S.sb("cbq1", [128, 8], F32)
    S.cbq2 = S.sb("cbq2", [128, 8], F32)
    S.cbk1 = S.sb("cbk1", [128, 2], F32)
    S.cbk2 = S.sb("cbk2", [128, 2], F32)
    S.qm = S.sb("qm", [128, TOK], BF16)
    S.km = S.sb("km", [128, TOK], BF16)
    S.rowsA = S.sb("rowsA", [1, TOK], F32)
    S.rowsB = S.sb("rowsB", [1, TOK], F32)
    S.c1neg = S.sb("c1neg", [128, 32], F32)
    S.a1pt = S.sb("a1pt", [1, 16], I32)

    scr = ctx.enter_context(nc.sbuf_tensor("pscratch", [128, 12384], mybir.dt.uint8))
    scr_base = nc.lookup_mloc(scr).addr
    S.sb_at = lambda name, shape, off, dt=BF16: nc.alloc_sbuf_tensor_at(
        name, shape, dt, offset=scr_base + off)
    S.tr0 = S.sb_at("tr0", [1, TOK], 0, F32)
    S.qrow = S.sb_at("qrow", [1, TOK], 4096, F32)
    S.trow = S.sb_at("trow", [1, TOK], 8192, F32)
    S.mx48 = S.sb_at("mx48", [1, 64], 4480, F32)
    S.l1mtmp = S.sb_at("l1mtmp", [1, 1], 4736, F32)

    S.p512 = [S.psum(f"p512_{k}", [128, 512]) for k in range(4)]
    S.pbig = [S.psum(f"pbig_{k}", [128, 512]) for k in range(3)]
    S.pbf = S.psum("pbf", [128, 1024], BF16)
    S.prow = [S.pbig[0][0:1, :], S.pbig[1][0:1, :], S.pbig[2][0:1, :],
              S.p512[3][0:1, :]]

    A = S.arena
    S.av12 = A[:, 0:12384].rearrange("p (c t) -> p c t", c=12)
    S.q0 = A[:, 0:8256].rearrange("p (c t) -> p c t", c=8)
    S.k0 = A[:, 8256:10320].rearrange("p (c t) -> p c t", c=2)
    S.v0 = A[:, 10320:12384].rearrange("p (c t) -> p c t", c=2)
    S.xb2 = A[:, 0:2048].rearrange("p (c t) -> p c t", c=2)
    S.sq2 = A[:, 2048:4096].rearrange("p (c t) -> p c t", c=2)
    S.q1 = S.xn[:, 0:8224].rearrange("p (c t) -> p c t", c=8)
    S.k1 = S.xn[:, 8224:10280].rearrange("p (c t) -> p c t", c=2)
    S.q2 = A[:, 0:8192].rearrange("p (c t) -> p c t", c=8)
    S.k2 = A[:, 8256:10304].rearrange("p (c t) -> p c t", c=2)
    return S


def finalize(S, phases):
    nc = S.nc
    for ph in phases:
        with nc.Block() as block:
            for engname in ENGS:
                if ph.prog[engname] or (engname == "sync" and ph.drain):
                    getattr(block, engname)(
                        (lambda ph_, n_: lambda e: ph_._emit(n_, e))(ph, engname))
        nc.all_engine_barrier()
    S.ctx.close()
    return nc



def make_p0(S):
    sc = Script(S, "p0")
    st = sc.step
    irow = S.sb_at("p0_irow", [1, 128], 0, F32)
    ktb = S.sb_at("p0_ktb", [1, 4], 512, F32)

    for c in range(NRES):
        st("sync", lambda S, e, c=c: e.dma_start(
            out=S.xs[:, c * TOK:(c + 1) * TOK],
            in_=S.x[c * 128:(c + 1) * 128, :]), dma=True)
    for c in range(NRES, NCH):
        st("sync", lambda S, e, c=c: e.dma_start(
            out=S.xpark[c - NRES], in_=S.x[c * 128:(c + 1) * 128, :]), dma=True)
    for dst, src in ((S.gcols, S.gamma), (S.nwcols, S.nwc), (S.scol, S.scales),
                     (S.bcol, S.fbias), (S.cbq1, S.cb_q1), (S.cbq2, S.cb_q2),
                     (S.cbk1, S.cb_k1), (S.cbk2, S.cb_k2)):
        st("sync", lambda S, e, dst=dst, src=src: e.dma_start(
            out=dst[:], in_=src[:]), dma=True)

    g_ir = st("gpsimd", lambda S, e: e.iota(
        irow.rearrange("o (a j ee) -> o a j ee", a=2, j=32)[:],
        pattern=[[0, 2], [1, 32], [0, 2]], base=0, channel_multiplier=0,
        allow_small_or_imprecise_dtypes=True))
    g_id = st("gpsimd", lambda S, e: e.iota(
        S.xn[:, 0:128], pattern=[[1, 128]], base=0, channel_multiplier=-1,
        allow_small_or_imprecise_dtypes=True))
    g_pc = st("gpsimd", lambda S, e: e.iota(
        S.posc[:, 0:1], pattern=[[0, 1]], base=0, channel_multiplier=1,
        allow_small_or_imprecise_dtypes=True))
    g_sg = st("gpsimd", lambda S, e: e.iota(
        S.signr.rearrange("p (j ee) -> p j ee", j=64)[:],
        pattern=[[0, 64], [1, 2]], base=0, channel_multiplier=0,
        allow_small_or_imprecise_dtypes=True))
    g_pb = st("gpsimd", lambda S, e: e.dma_start(
        out=S.pbr[:], in_=S.pbase[0:1, 0:12].broadcast_to([128, 12])), dma=True)
    g_hz = st("gpsimd", lambda S, e: e.dma_start(
        out=S.hzr[:], in_=S.hz[:]), dma=True)
    g_kt = st("gpsimd", lambda S, e: e.dma_start(
        out=ktb[:, 0:1], in_=S.key_temp[:]), dma=True)

    st("vector", lambda S, e: e.memset(S.ones_c[:], 1.0))
    st("vector", lambda S, e: e.memset(S.ones_cf[:], 1.0))
    st("vector", lambda S, e: e.memset(S.ones_r32[:], 1.0))
    st("vector", lambda S, e: e.tensor_scalar(
        out=S.identb[:], in0=S.xn[:, 0:128], scalar1=0.0, scalar2=None,
        op0=ALU.is_equal), deps=[g_id])
    st("vector", lambda S, e: e.tensor_scalar(
        out=S.identf[:], in0=S.xn[:, 0:128], scalar1=0.0, scalar2=None,
        op0=ALU.is_equal))
    st("vector", lambda S, e: e.tensor_scalar(
        out=S.signr[:], in0=S.signr[:], scalar1=-2.0, scalar2=1.0,
        op0=ALU.mult, op1=ALU.add), deps=[g_sg])
    st("vector", lambda S, e: e.tensor_scalar(
        out=ktb[:, 3:4], in0=ktb[:, 0:1], scalar1=0.125, scalar2=None,
        op0=ALU.mult), deps=[g_kt])
    st("vector", lambda S, e: e.scalar_tensor_tensor(
        out=ktb[:, 0:1], in0=ktb[:, 0:1], scalar=-1.0, in1=ktb[:, 0:1],
        op0=ALU.mult, op1=ALU.max))
    st("vector", lambda S, e: e.tensor_scalar(
        out=ktb[:, 0:1], in0=ktb[:, 0:1], scalar1=0.125 * 1.02, scalar2=None,
        op0=ALU.mult))
    st("vector", lambda S, e: e.tensor_scalar(
        out=ktb[:, 1:2], in0=ktb[:, 0:1], scalar1=-1.0, scalar2=None,
        op0=ALU.mult))
    v_kt = st("vector", lambda S, e: e.memset(ktb[:, 2:3], 0.0))
    for t in range(7, -1, -1):
        st("vector", lambda S, e, t=t: e.tensor_tensor(
            out=S.posc[:, t:t + 1], in0=S.posc[:, 0:1], in1=S.pbr[:, t:t + 1],
            op=ALU.add), deps=[g_pc, g_pb])

    s_exp = st("scalar", lambda S, e: e.activation(
        out=irow[:], in_=irow[:], func=AF.Exp,
        scale=-math.log(10000.0) / 32.0), deps=[g_ir])

    rA = st("sync", lambda S, e: e.dma_start(
        out=S.rt[0:1, 0:128], in_=irow[:]), deps=[s_exp], dma=True)
    rB = st("sync", lambda S, e: e.dma_start(
        out=S.rt[1:2, 0:4], in_=ktb[:]), deps=[v_kt], dma=True)
    st("gpsimd", lambda S, e: e.dma_start(
        out=S.invfr[:], in_=S.rt[0:1, 0:128].broadcast_to([128, 128])),
        deps=[rA], dma=True)
    st("gpsimd", lambda S, e: e.dma_start(
        out=S.ktep[:], in_=S.rt[1:2, 3:4].broadcast_to([128, 1])),
        deps=[rB], dma=True)
    st("gpsimd", lambda S, e: e.dma_start(
        out=S.nktc[:], in_=S.rt[1:2, 1:2].broadcast_to([128, 1])),
        deps=[rB], dma=True)
    return sc



def make_lnstats(S, tag, mode="ln"):
    sc = Script(S, tag)
    st = sc.step
    ctok, sqtok, mmx, mmq, pk = {}, {}, {}, {}, {}
    for c in range(NCH):
        if c >= NRES:
            pk[c] = st("sync", lambda S, e, c=c: e.dma_start(
                out=S.xw[c % 2][:], in_=S.xpark[c - NRES]),
                deps=[ctok.get(c - 2) if c - 2 >= NRES else None,
                      sqtok.get(c - 2) if c - 2 >= NRES else None], dma=True)
        src = (S.xs[:, c * TOK:(c + 1) * TOK] if c < NRES else S.xw[c % 2][:])
        ctok[c] = st("vector", lambda S, e, c=c, src=src: e.tensor_copy(
            S.xb2[:, c % 2, :], src), deps=[pk.get(c), mmx.get(c - 2)])
        sqtok[c] = st("scalar", lambda S, e, c=c, src=src: e.activation(
            out=S.sq2[:, c % 2, :], in_=src, func=AF.Square),
            deps=[pk.get(c), mmq.get(c - 2)])
        st("tensor", lambda S, e, c=c: e.matmul(
            S.prow[0], S.ones_c[:], S.xb2[:, c % 2, 0:512],
            start=(c == 0), stop=(c == NCH - 1)), deps=[ctok[c]])
        mmx[c] = st("tensor", lambda S, e, c=c: e.matmul(
            S.prow[1], S.ones_c[:], S.xb2[:, c % 2, 512:1024],
            start=(c == 0), stop=(c == NCH - 1)))
        st("tensor", lambda S, e, c=c: e.matmul(
            S.prow[2], S.ones_c[:], S.sq2[:, c % 2, 0:512],
            start=(c == 0), stop=(c == NCH - 1)), deps=[sqtok[c]])
        mmq[c] = st("tensor", lambda S, e, c=c: e.matmul(
            S.prow[3], S.ones_c[:], S.sq2[:, c % 2, 512:1024],
            start=(c == 0), stop=(c == NCH - 1)))

    v = lambda fn, deps=(): st("vector", fn, deps=deps)
    v(lambda S, e: e.tensor_copy(S.tr0[:, 0:512], S.prow[0]), deps=[mmx[NCH - 1]])
    v(lambda S, e: e.tensor_copy(S.tr0[:, 512:1024], S.prow[1]))
    v(lambda S, e: e.tensor_copy(S.qrow[:, 0:512], S.prow[2]), deps=[mmq[NCH - 1]])
    v(lambda S, e: e.tensor_copy(S.qrow[:, 512:1024], S.prow[3]))
    v(lambda S, e: e.tensor_scalar(out=S.tr0[:], in0=S.tr0[:], scalar1=1.0 / D,
                                   scalar2=None, op0=ALU.mult))
    v(lambda S, e: e.tensor_scalar(out=S.qrow[:], in0=S.qrow[:], scalar1=1.0 / D,
                                   scalar2=None, op0=ALU.mult))
    v(lambda S, e: e.tensor_tensor(out=S.rowsB[:], in0=S.tr0[:], in1=S.tr0[:],
                                   op=ALU.mult))
    v(lambda S, e: e.tensor_tensor(out=S.qrow[:], in0=S.qrow[:], in1=S.rowsB[:],
                                   op=ALU.subtract))
    tvar = v(lambda S, e: e.tensor_scalar(out=S.qrow[:], in0=S.qrow[:],
                                          scalar1=EPS_LN, scalar2=None, op0=ALU.add))
    sA = st("scalar", lambda S, e: e.activation(out=S.rowsA[:], in_=S.qrow[:],
                                                func=AF.Sqrt), deps=[tvar])
    v(lambda S, e: e.reciprocal(out=S.rowsA[:], in_=S.rowsA[:]), deps=[sA])
    v(lambda S, e: e.tensor_tensor(out=S.rowsB[:], in0=S.tr0[:], in1=S.rowsA[:],
                                   op=ALU.mult))
    v(lambda S, e: e.tensor_scalar(out=S.rowsB[:], in0=S.rowsB[:], scalar1=-1.0,
                                   scalar2=None, op0=ALU.mult))
    if mode == "ln+rms":
        v(lambda S, e: e.tensor_scalar(out=S.qrow[:], in0=S.qrow[:],
                                       scalar1=-EPS_LN, scalar2=None, op0=ALU.add))
        v(lambda S, e: e.tensor_tensor(out=S.trow[:], in0=S.tr0[:], in1=S.tr0[:],
                                       op=ALU.mult))
        v(lambda S, e: e.tensor_tensor(out=S.qrow[:], in0=S.qrow[:], in1=S.trow[:],
                                       op=ALU.add))
        v(lambda S, e: e.tensor_tensor(out=S.trow[:], in0=S.rowsA[:], in1=S.rowsA[:],
                                       op=ALU.mult))
        v(lambda S, e: e.tensor_tensor(out=S.qrow[:], in0=S.qrow[:], in1=S.trow[:],
                                       op=ALU.mult))
        v(lambda S, e: e.tensor_tensor(out=S.trow[:], in0=S.rowsA[:], in1=S.rowsB[:],
                                       op=ALU.mult))
        v(lambda S, e: e.tensor_tensor(out=S.trow[:], in0=S.trow[:], in1=S.tr0[:],
                                       op=ALU.mult))
        v(lambda S, e: e.tensor_scalar(out=S.trow[:], in0=S.trow[:], scalar1=2.0,
                                       scalar2=None, op0=ALU.mult))
        v(lambda S, e: e.tensor_tensor(out=S.qrow[:], in0=S.qrow[:], in1=S.trow[:],
                                       op=ALU.add))
        v(lambda S, e: e.tensor_tensor(out=S.trow[:], in0=S.rowsB[:], in1=S.rowsB[:],
                                       op=ALU.mult))
        v(lambda S, e: e.tensor_tensor(out=S.qrow[:], in0=S.qrow[:], in1=S.trow[:],
                                       op=ALU.add))
        trms = v(lambda S, e: e.tensor_scalar(out=S.qrow[:], in0=S.qrow[:],
                                              scalar1=EPS_RMS, scalar2=None,
                                              op0=ALU.add))
        sT = st("scalar", lambda S, e: e.activation(out=S.trow[:], in_=S.qrow[:],
                                                    func=AF.Sqrt), deps=[trms])
        v(lambda S, e: e.reciprocal(out=S.trow[:], in_=S.trow[:]), deps=[sT])
        v(lambda S, e: e.tensor_tensor(out=S.rowsA[:], in0=S.rowsA[:],
                                       in1=S.trow[:], op=ALU.mult))
        v(lambda S, e: e.tensor_tensor(out=S.rowsB[:], in0=S.rowsB[:],
                                       in1=S.trow[:], op=ALU.mult))
    return sc



def _apw(ap):
    n = 1
    for d_ in ap.shape[1:]:
        n *= d_
    return n


def evac_residual(scale=None, bias=None):
    state = {"n": 0, "store": {}}

    def evac(S, sc, half, j, oc, deps):
        st = sc.step
        ps = S.p512[j % 2]
        lo = OWNL[half]
        pkdep = []
        if oc < NRES:
            dst = S.xs[:, oc * TOK + lo: oc * TOK + lo + 512]
        else:
            n = state["n"]
            state["n"] += 1
            sl = n % 2
            pk = st("sync", lambda S, e, sl=sl, oc=oc, lo=lo: e.dma_start(
                out=S.xr[sl][:], in_=S.xpark[oc - NRES, :, lo:lo + 512]),
                deps=[state["store"].get(n - 2)], dma=True)
            dst = S.xr[sl][:]
            pkdep = [pk]
        if scale is None:
            v1 = st("vector", lambda S, e, ps=ps, dst=dst: e.tensor_tensor(
                out=dst, in0=ps[:], in1=dst, op=ALU.add), deps=list(deps) + pkdep)
        else:
            v1 = st("vector", lambda S, e, ps=ps, dst=dst, oc=oc: e.scalar_tensor_tensor(
                out=dst, in0=ps[:], scalar=scale(S, oc), in1=dst,
                op0=ALU.mult, op1=ALU.add), deps=list(deps) + pkdep)
        last = v1
        if bias is not None:
            last = st("vector", lambda S, e, dst=dst, oc=oc: e.tensor_single_scalar(
                dst, dst, bias(S, oc), ALU.add))
        if oc >= NRES:
            state["store"][n] = st("sync", lambda S, e, sl=sl, oc=oc, lo=lo: e.dma_start(
                out=S.xpark[oc - NRES, :, lo:lo + 512], in_=S.xr[sl][:]),
                deps=[last], dma=True)
        return v1
    return evac


def evac_qkv():
    def evac(S, sc, half, j, oc, deps):
        st = sc.step
        ps = S.p512[j % 2]
        base = 516 * half + 4
        c1 = st("vector", lambda S, e, ps=ps, oc=oc, base=base: e.tensor_copy(
            S.av12[:, oc, base:base + 512], ps[:]), deps=deps)
        if oc >= 10:
            return c1
        acc = S.xr[0] if oc < 8 else S.xr[1]
        if oc in (0, 8):
            a1 = st("vector", lambda S, e, ps=ps, acc=acc: e.tensor_copy(
                acc[:], ps[:]))
        else:
            a1 = st("vector", lambda S, e, ps=ps, acc=acc: e.tensor_tensor(
                out=acc[:], in0=acc[:], in1=ps[:], op=ALU.add))
        if oc == 7:
            st("vector", lambda S, e, half=half: e.tensor_copy(
                S.qm[:, half * 512:half * 512 + 512], S.xr[0][:]))
        elif oc == 9:
            st("vector", lambda S, e, half=half: e.tensor_copy(
                S.km[:, half * 512:half * 512 + 512], S.xr[1][:]))
        return a1
    return evac


def make_proj(S, jobs, tag, post_nw=False, weng="gpsimd"):
    sc = Script(S, tag)
    st = sc.step
    noc = sum(jb[1] for jb in jobs)
    mm_first, mm_last, ev, atok = {}, {}, {}, {}
    j = 0
    for half in range(2):
        lo = OWNL[half]
        bA = st("tensor", lambda S, e, lo=lo: e.matmul(
            S.p512[2][:], S.ones_r32[:], S.rowsA[:, lo:lo + 512],
            start=True, stop=True),
            deps=[atok.get((0, NCH - 1))] if half == 1 else [])
        bB = st("tensor", lambda S, e, lo=lo: e.matmul(
            S.p512[3][:], S.ones_r32[:], S.rowsB[:, lo:lo + 512],
            start=True, stop=True))
        for c in range(NCH):
            deps = [bB]
            if c >= NRES:
                n = half * NPARK + (c - NRES)
                wdep = None
                if n >= 2:
                    h2, c2 = (n - 2) // NPARK, NRES + (n - 2) % NPARK
                    wdep = atok[(h2, c2)]
                ap = st("sync", lambda S, e, c=c: e.dma_start(
                    out=S.xw[c % 2][:], in_=S.xpark[c - NRES]),
                    deps=[wdep], dma=True)
                deps.append(ap)
            if half == 1:
                deps.append(mm_last.get(noc - 1))
            xsl = (S.xs[:, c * TOK + lo: c * TOK + lo + 512] if c < NRES
                   else S.xw[c % 2][:, lo:lo + 512])
            nsl = S.xn[:, c * 512:(c + 1) * 512]
            st("vector", lambda S, e, nsl=nsl, xsl=xsl: e.tensor_tensor(
                out=nsl, in0=xsl, in1=S.p512[2][:], op=ALU.mult), deps=deps)
            a2 = st("vector", lambda S, e, nsl=nsl: e.tensor_tensor(
                out=nsl, in0=nsl, in1=S.p512[3][:], op=ALU.add))
            if post_nw:
                a2 = st("vector", lambda S, e, nsl=nsl, c=c: e.tensor_single_scalar(
                    nsl, nsl, S.nwcols[:, c:c + 1], ALU.mult))
            atok[(half, c)] = a2
        for slabs, n_oc, evac in jobs:
            for oc in range(n_oc):
                s = j % 2
                ap = slabs[oc]
                kch = _apw(ap) // 128
                wb = st(weng, lambda S, e, s=s, ap=ap, w=_apw(ap): e.dma_start(
                    out=S.wb[s][:, 0:w], in_=ap),
                    deps=[mm_first.get(j - 1)], dma=True)
                for c in range(kch):
                    deps = [wb, atok[(half, c)]]
                    if c == 0:
                        deps.append(ev.get(j - 2))
                    mm = st("tensor", lambda S, e, s=s, c=c, j=j, kch=kch: e.matmul(
                        S.p512[j % 2][:], S.wb[s][:, c * 128:(c + 1) * 128],
                        S.xn[:, c * 512:(c + 1) * 512],
                        start=(c == 0), stop=(c == kch - 1)), deps=deps)
                    if c == 0:
                        mm_first[j] = mm
                mm_last[j] = mm
                ev[j] = evac(S, sc, half, j, oc, deps=[mm_last[j]])
                j += 1
    return sc


def make_ao_proj(S, slab, n_oc, kch, src, tag, weng="gpsimd"):
    sc = Script(S, tag)
    st = sc.step
    evac = evac_residual()
    mm_first, mm_last, ev = {}, {}, {}
    j = 0
    for half in range(2):
        lo = OWNL[half]
        rtok = {}
        for c in range(kch):
            rtok[c] = st("sync", lambda S, e, c=c, lo=lo: e.dma_start(
                out=S.xn[:, c * 512:(c + 1) * 512], in_=src[c, :, lo:lo + 512]),
                deps=[mm_last.get(n_oc - 1)] if half == 1 else [], dma=True)
        for oc in range(n_oc):
            s = j % 2
            wb = st(weng, lambda S, e, s=s, oc=oc: e.dma_start(
                out=S.wb[s][:, 0:kch * 128], in_=slab[oc]),
                deps=[mm_first.get(j - 1)], dma=True)
            for c in range(kch):
                deps = [wb, rtok[c]]
                if c == 0:
                    deps.append(ev.get(j - 2))
                mm = st("tensor", lambda S, e, s=s, c=c, j=j: e.matmul(
                    S.p512[j % 2][:], S.wb[s][:, c * 128:(c + 1) * 128],
                    S.xn[:, c * 512:(c + 1) * 512],
                    start=(c == 0), stop=(c == kch - 1)), deps=deps)
                if c == 0:
                    mm_first[j] = mm
            mm_last[j] = mm
            ev[j] = evac(S, sc, half, j, oc, deps=[mm_last[j]])
            j += 1
    return sc


def make_l1proj(S, weng="gpsimd"):
    sc = Script(S, "l1p")
    st = sc.step
    jobs = [
        (S.l1qT, 32, lambda S_, half, oc, lo: S_.q1d[oc, :, lo:lo + 512]),
        (S.l1kT, 8, lambda S_, half, oc, lo: S_.k1c[half, oc]),
        (S.l1vT, 8, lambda S_, half, oc, lo: S_.v1f[oc, :, lo:lo + 512]),
    ]
    stg = [S.arena[:, k * 512:(k + 1) * 512] for k in range(2)]
    mm_first, mm_last, ev, atok, evo = {}, {}, {}, {}, {}
    j = 0
    for half in range(2):
        lo = OWNL[half]
        bA = st("tensor", lambda S, e, lo=lo: e.matmul(
            S.p512[2][:], S.ones_r32[:], S.rowsA[:, lo:lo + 512],
            start=True, stop=True),
            deps=[atok.get((0, NCH - 1))] if half == 1 else [])
        bB = st("tensor", lambda S, e, lo=lo: e.matmul(
            S.p512[3][:], S.ones_r32[:], S.rowsB[:, lo:lo + 512],
            start=True, stop=True))
        for c in range(NCH):
            deps = [bB]
            if c >= NRES:
                n = half * NPARK + (c - NRES)
                wdep = None
                if n >= 2:
                    h2, c2 = (n - 2) // NPARK, NRES + (n - 2) % NPARK
                    wdep = atok[(h2, c2)]
                ap = st("sync", lambda S, e, c=c: e.dma_start(
                    out=S.xw[c % 2][:], in_=S.xpark[c - NRES]),
                    deps=[wdep], dma=True)
                deps.append(ap)
            if half == 1:
                deps.append(mm_last.get(47))
            xsl = (S.xs[:, c * TOK + lo: c * TOK + lo + 512] if c < NRES
                   else S.xw[c % 2][:, lo:lo + 512])
            nsl = S.xn[:, c * 512:(c + 1) * 512]
            st("vector", lambda S, e, nsl=nsl, xsl=xsl: e.tensor_tensor(
                out=nsl, in0=xsl, in1=S.p512[2][:], op=ALU.mult), deps=deps)
            atok[(half, c)] = st("vector", lambda S, e, nsl=nsl: e.tensor_tensor(
                out=nsl, in0=nsl, in1=S.p512[3][:], op=ALU.add))
        for ji, (slab, n_oc, dstd) in enumerate(jobs):
            for oc in range(n_oc):
                s = j % 2
                wb = st(weng, lambda S, e, s=s, slab=slab, oc=oc: e.dma_start(
                    out=S.wb[s][:], in_=slab[oc]),
                    deps=[mm_first.get(j - 1)], dma=True)
                for c in range(NCH):
                    deps = [wb, atok[(half, c)]]
                    if c == 0:
                        deps.append(ev.get(j - 2))
                    mm = st("tensor", lambda S, e, s=s, c=c, j=j: e.matmul(
                        S.p512[j % 2][:], S.wb[s][:, c * 128:(c + 1) * 128],
                        S.xn[:, c * 512:(c + 1) * 512],
                        start=(c == 0), stop=(c == NCH - 1)), deps=deps)
                    if c == 0:
                        mm_first[j] = mm
                mm_last[j] = mm
                v1 = st("vector", lambda S, e, j=j: e.tensor_copy(
                    stg[j % 2], S.p512[j % 2][:]),
                    deps=[mm_last[j], evo.get(j - 2)])
                ev[j] = v1
                evo[j] = st("sync", lambda S, e, half=half, dstd=dstd, oc=oc, j=j, lo=lo:
                            e.dma_start(out=dstd(S, half, oc, lo), in_=stg[j % 2]),
                            deps=[v1], dma=True)
                j += 1
    return sc



def make_halocc(S):
    sc = Script(S, "halo")
    st = sc.step
    hqgf = S.hqg.rearrange("r s c p f -> (r s c) p f")
    sts = []
    for sl in range(2):
        for cc in range(12):
            sts.append(st("sync", lambda S, e, sl=sl, cc=cc: e.dma_start(
                out=S.hqc[sl, cc], in_=S.av12[:, cc, 516 * sl + 512:516 * sl + 516]),
                dma=True))
    ag = st("gpsimd", lambda S, e: e.collective_compute(
        "AllGather", ALU.bypass, replica_groups=[list(range(8))],
        ins=[S.hqc[:]], outs=[S.hqg[:]]), deps=sts, coll=True)
    pl = st("gpsimd", lambda S, e: e.dma_start(out=S.a1pt[:], in_=S.prm[:]), dma=True)
    for qb in range(2):
        stage = S.xn[:, qb * 64:qb * 64 + 48]

        def _ld(S, e, qb=qb, stage=stage):
            off = e.value_load(S.a1pt[0:1, qb:qb + 1])
            return e.dma_start(
                out=stage.rearrange("p (c f) -> p c f", c=12),
                in_=hqgf[bass.ds(off, 12)].rearrange("c p f -> p c f"))
        ld = st("sync", _ld, deps=[ag, pl], dma=True)
        mk = st("vector", lambda S, e, qb=qb, stage=stage: e.tensor_single_scalar(
            stage, stage, S.hzr[:, qb:qb + 1], ALU.mult), deps=[ld])
        for cc in range(12):
            st("vector", lambda S, e, qb=qb, cc=cc, stage=stage: e.tensor_copy(
                S.av12[:, cc, 516 * qb:516 * qb + 4], stage[:, 4 * cc:4 * cc + 4]),
                deps=[mk])
    return sc


def q1dst(S, oc, b, kind):
    if kind == "pre":
        return S.q1[:, oc, 514 * b:514 * b + 2]
    return S.q1[:, oc, 514 * b + 2:514 * b + 514]


def k1dst(S, oc, b, kind):
    if kind == "pre":
        return S.k1[:, oc, 514 * b:514 * b + 2]
    return S.k1[:, oc, 514 * b + 2:514 * b + 514]


def q2dst(S, oc, b, kind):
    return S.q2[:, oc, 512 * b:512 * b + 512]


def k2dst(S, oc, b, kind):
    return S.k2[:, oc, 512 * b:512 * b + 512]


def make_conv(S, jobs, pre2, tag):
    sc = Script(S, tag)
    st = sc.step
    pown = [S.p512[0], S.p512[1]]
    ppre = [S.p512[2][:, 0:2], S.p512[3][:, 0:2]] if pre2 else None
    mm_first, lastev = {}, {}
    j = 0
    for slabs, n_oc, kch, src, dst, bias, ibase in jobs:
        for oc in range(n_oc):
            s = j % 2
            ap = slabs(oc)
            wb = st("gpsimd", lambda S, e, s=s, ap=ap, w=_apw(ap): e.dma_start(
                out=S.wb[s][:, 0:w], in_=ap),
                deps=[mm_first.get(j - 1)], dma=True)
            gi = 0
            for b in range(2):
                segs = ([("pre", 2)] if pre2 else []) + [("own", 512)]
                for kind, wd in segs:
                    ps = (pown if kind == "own" else ppre)[j % 2]
                    nmm = 0
                    for tau in range(3):
                        for c in range(kch):
                            ilo = ibase(b) + tau + (2 if (kind == "own" and pre2) else 0)
                            deps = [wb]
                            if nmm == 0:
                                deps.append(lastev.get((kind, j % 2)))
                            mm = st("tensor", lambda S, e, s=s, ps=ps, wd=wd, tau=tau,
                                    c=c, kch=kch, ilo=ilo, nmm=nmm, oc=oc, src=src:
                                    e.matmul(ps[:, 0:wd],
                                             S.wb[s][:, (tau * kch + c) * 128:
                                                     (tau * kch + c + 1) * 128],
                                             src(S, c, oc)[:, ilo:ilo + wd],
                                             start=(nmm == 0),
                                             stop=(nmm == 3 * kch - 1)), deps=deps)
                            if j not in mm_first:
                                mm_first[j] = mm
                            nmm += 1
                    lastev[(kind, j % 2)] = st(
                        "vector", lambda S, e, ps=ps, wd=wd, oc=oc,
                        b=b, kind=kind, dst=dst, bias=bias:
                        e.tensor_single_scalar(
                            dst(S, oc, b, kind), ps[:, 0:wd],
                            bias(S, oc), ALU.add), deps=[mm])
                    gi += 1
            j += 1
    return sc


def make_meanfix(S):
    sc = Script(S, "mf")
    st = sc.step
    for bi, (buf, scale) in enumerate(((S.qm, 1.0 / 16.0), (S.km, 1.0 / 4.0))):
        stage = S.xn[0:64, bi * TOK:(bi + 1) * TOK]
        t1 = st("sync", lambda S, e, buf=buf, stage=stage: e.dma_start(
            out=stage, in_=buf[64:128, :]), dma=True)
        st("vector", lambda S, e, buf=buf, stage=stage: e.tensor_tensor(
            out=buf[0:64, :], in0=buf[0:64, :], in1=stage, op=ALU.add), deps=[t1])
        t3 = st("vector", lambda S, e, buf=buf, scale=scale: e.tensor_scalar(
            out=buf[0:64, :], in0=buf[0:64, :], scalar1=scale, scalar2=None,
            op0=ALU.mult))
        st("sync", lambda S, e, buf=buf: e.dma_start(
            out=buf[64:128, :], in_=buf[0:64, :]), deps=[t3], dma=True)
    return sc



def make_postproc(S):
    sc = Script(S, "pp")
    st = sc.step
    id32 = S.identf
    hs32 = S.sb_at("pp_hs32", [128, 2], 0, F32)
    hsT32 = S.sb_at("pp_hsT32", [2, 128], 32, F32)
    rn = S.sb_at("pp_rn", [2, 128], 544, F32)
    i32t = S.sb_at("pp_i32", [128, 128], 1056, I32)
    angf, angc = S.xr[0], S.xr[1]
    qxf, CCf = S.xw[0], S.xw[1]
    stg = S.xn[:, 0:1280].rearrange("p (c t) -> p c t", c=10)
    vtm = S.xn[:, 1280:1540]
    vtmv = S.xn[:, 1280:1540].rearrange("p (h u) -> p h u", u=65)
    pn2 = S.pbig[0][0:2, 0:128]
    pbc = S.pbig[1][:, 0:128]
    psT = S.pbig[2][:, 0:128]
    psV = S.pbf[:, 0:128]

    st("gpsimd", lambda S, e: e.memset(hs32[:], 0.0))
    st("gpsimd", lambda S, e: e.memset(hs32[0:64, 0:1], 1.0))
    g3 = st("gpsimd", lambda S, e: e.memset(hs32[64:128, 1:2], 1.0))
    T0 = st("tensor", lambda S, e: e.transpose(pn2, hs32[:], id32[:]), deps=[g3])
    e0 = st("vector", lambda S, e: e.tensor_copy(hsT32[:], pn2), deps=[T0])

    swm = []
    for i_ in range(16):
        swm += [2 * i_ + 1, 2 * i_]

    prev_h3, prev_h6 = e0, None
    prev_T1 = prev_T2 = prev_e2 = None
    prevV = None
    prev_vst = None
    prev_store = {}
    for t in range(8):
        qb, tl = t // 4, t % 4
        col = qb * 512 + tl * 128
        st("vector", lambda S, e, t=t: e.tensor_single_scalar(
            angf[:, 0:128], S.invfr[:], S.posc[:, t:t + 1], ALU.mult),
            deps=[prev_T2])
        st("vector", lambda S, e: e.tensor_scalar(
            out=i32t[:], in0=angf[:, 0:128], scalar1=1.0 / TWO_PI, scalar2=None,
            op0=ALU.mult))
        st("vector", lambda S, e: e.scalar_tensor_tensor(
            out=angf[:, 128:256], in0=i32t[:], scalar=-TWO_PI, in1=angf[:, 0:128],
            op0=ALU.mult, op1=ALU.add))
        st("vector", lambda S, e: e.tensor_scalar(
            out=angf[:, 128:256], in0=angf[:, 128:256], scalar1=0.0,
            scalar2=TWO_PI - 1e-5, op0=ALU.max, op1=ALU.min))
        a5 = st("vector", lambda S, e: e.tensor_scalar(
            out=angf[:, 128:256], in0=angf[:, 128:256], scalar1=PI, scalar2=None,
            op0=ALU.subtract))
        s1 = st("scalar", lambda S, e: e.activation(
            out=angf[:, 256:384], in_=angf[:, 128:256], func=AF.Sin), deps=[a5])
        st("vector", lambda S, e: e.tensor_scalar(
            out=angc[:, 0:128], in0=angf[:, 0:128], scalar1=PI / 2.0, scalar2=None,
            op0=ALU.add), deps=[prev_T1])
        st("vector", lambda S, e: e.tensor_scalar(
            out=i32t[:], in0=angc[:, 0:128], scalar1=1.0 / TWO_PI, scalar2=None,
            op0=ALU.mult))
        st("vector", lambda S, e: e.scalar_tensor_tensor(
            out=angc[:, 128:256], in0=i32t[:], scalar=-TWO_PI, in1=angc[:, 0:128],
            op0=ALU.mult, op1=ALU.add))
        st("vector", lambda S, e: e.tensor_scalar(
            out=angc[:, 128:256], in0=angc[:, 128:256], scalar1=0.0,
            scalar2=TWO_PI - 1e-5, op0=ALU.max, op1=ALU.min))
        c5 = st("vector", lambda S, e: e.tensor_scalar(
            out=angc[:, 128:256], in0=angc[:, 128:256], scalar1=PI, scalar2=None,
            op0=ALU.subtract))
        s2 = st("scalar", lambda S, e: e.activation(
            out=angc[:, 256:384], in_=angc[:, 128:256], func=AF.Sin), deps=[c5])
        b1 = st("vector", lambda S, e: e.tensor_tensor(
            out=angf[:, 384:512], in0=angf[:, 256:384], in1=S.signr[:],
            op=ALU.mult), deps=[s1, prev_T2])
        b2 = st("vector", lambda S, e: e.tensor_scalar(
            out=angc[:, 384:512], in0=angc[:, 256:384], scalar1=-1.0, scalar2=None,
            op0=ALU.mult), deps=[s2, prev_T1])
        T1 = st("tensor", lambda S, e: e.transpose(psT, angc[:, 384:512], id32[:]),
                deps=[b2, prev_e2])
        e1 = st("vector", lambda S, e: e.tensor_copy(CCf[:, 0:128], psT), deps=[T1])
        T2 = st("tensor", lambda S, e: e.transpose(psT, angf[:, 384:512], id32[:]),
                deps=[b1, e1])
        e2 = st("vector", lambda S, e: e.tensor_copy(CCf[:, 128:256], psT), deps=[T2])
        prev_T1, prev_T2, prev_e2 = T1, T2, e2
        for cc in range(10):
            is_q = cc < 8
            src = (S.q2 if is_q else S.k2)[:, cc if is_q else cc - 8, col:col + 128]
            mean = (S.km if is_q else S.qm)[:, col:col + 128]
            st("vector", lambda S, e, src=src, mean=mean: e.scalar_tensor_tensor(
                out=qxf[:, 0:128], in0=mean, scalar=0.5, in1=src,
                op0=ALU.mult, op1=ALU.add))
            h2 = st("vector", lambda S, e: e.tensor_tensor(
                out=qxf[:, 128:256], in0=qxf[:, 0:128], in1=qxf[:, 0:128],
                op=ALU.mult))
            M1 = st("tensor", lambda S, e: e.matmul(
                pn2, hs32[:], qxf[:, 128:256], start=True, stop=True),
                deps=[h2, prev_h3])
            h3 = st("vector", lambda S, e: e.tensor_scalar(
                out=rn[:], in0=pn2, scalar1=1e-24, scalar2=None, op0=ALU.max),
                deps=[M1])
            h4 = st("vector", lambda S, e: e.reciprocal(out=rn[:], in_=rn[:]))
            s3 = st("scalar", lambda S, e: e.activation(out=rn[:], in_=rn[:],
                                                        func=AF.Sqrt), deps=[h4])
            if not is_q:
                s3 = st("vector", lambda S, e: e.tensor_single_scalar(
                    rn[:], rn[:], S.ktep[0:2, 0:1], ALU.mult), deps=[s3])
            M2 = st("tensor", lambda S, e: e.matmul(
                pbc, hsT32[:], rn[:], start=True, stop=True),
                deps=[s3] + ([prev_h6] if prev_h6 else []))
            h6 = st("vector", lambda S, e: e.tensor_tensor(
                out=qxf[:, 0:128], in0=qxf[:, 0:128], in1=pbc, op=ALU.mult),
                deps=[M2])
            st("vector", lambda S, e: e.stream_shuffle(
                out=qxf[:, 128:256], in_=qxf[:, 0:128], mask=swm))
            st("vector", lambda S, e: e.tensor_tensor(
                out=qxf[:, 0:128], in0=qxf[:, 0:128], in1=CCf[:, 0:128],
                op=ALU.mult), deps=[e1])
            st("vector", lambda S, e: e.tensor_tensor(
                out=qxf[:, 128:256], in0=qxf[:, 128:256], in1=CCf[:, 128:256],
                op=ALU.mult), deps=[e2])
            h10 = st("vector", lambda S, e, cc=cc: e.tensor_tensor(
                out=stg[:, cc, :], in0=qxf[:, 0:128], in1=qxf[:, 128:256],
                op=ALU.add), deps=prev_store.get(cc, []))
            prev_h3, prev_h6 = h3, h6
            if is_q:
                prev_store[cc] = [st("sync", lambda S, e, cc=cc, col=col: e.dma_start(
                    out=S.qfd[cc, :, col:col + 128], in_=stg[:, cc, :]),
                    deps=[h10], dma=True)]
            else:
                kc = cc - 8
                s_a = st("sync", lambda S, e, kc=kc, col=col, cc=cc: e.dma_start(
                    out=S.k0c[2 * kc, :, col:col + 128],
                    in_=stg[0:64, cc, :]), deps=[h10], dma=True)
                s_b = st("sync", lambda S, e, kc=kc, col=col, cc=cc: e.dma_start(
                    out=S.k0c[2 * kc + 1, :, col:col + 128],
                    in_=stg[64:128, cc, :]), deps=[h10], dma=True)
                prev_store[cc] = [s_a, s_b]
        vton = st("vector", lambda S, e: e.memset(vtmv[:, 0:4, 64:65], 1.0),
                  deps=[prev_vst])
        cend = None
        for vc in range(2):
            scol = 4 + 516 * qb + tl * 128 - (1 if vc == 1 else 0)
            T3 = st("tensor", lambda S, e, vc=vc, scol=scol: e.transpose(
                psV, S.v0[:, vc, scol:scol + 128], S.identb[:]), deps=[prevV])
            c1 = st("vector", lambda S, e, vc=vc: e.tensor_copy(
                vtm[:, 65 * (2 * vc):65 * (2 * vc) + 64], psV[:, 0:64]),
                deps=[T3, prev_vst])
            cend = st("vector", lambda S, e, vc=vc: e.tensor_copy(
                vtm[:, 65 * (2 * vc + 1):65 * (2 * vc + 1) + 64], psV[:, 64:128]))
            prevV = cend
        prev_vst = st("sync", lambda S, e, col=col: e.dma_start(
            out=S.v0c[col:col + 128, :], in_=vtm[:, 0:260]),
            deps=[cend, vton], dma=True)
    return sc


def make_cc1(S):
    sc = Script(S, "cc1")
    sc.step("gpsimd", lambda S, e: e.collective_compute(
        "AllGather", ALU.bypass, replica_groups=[list(range(8))],
        ins=[S.k0c[:]], outs=[S.k0g[:]]), coll=True)
    sc.step("gpsimd", lambda S, e: e.collective_compute(
        "AllGather", ALU.bypass, replica_groups=[list(range(8))],
        ins=[S.v0c[:]], outs=[S.v0g[:]]), coll=True)
    return sc



def make_attn0(S):
    sc = Script(S, "at0")
    st = sc.step
    qt = [S.xn[0:64, i * 512:(i + 1) * 512] for i in range(4)]
    kb = [S.xn[0:64, 2048 + i * 128:2048 + (i + 1) * 128] for i in range(2)]
    vb = [S.xn[:, 2304 + i * 65:2304 + (i + 1) * 65] for i in range(2)]
    Pb = [S.xn[:, 2560 + i * 512:2560 + (i + 1) * 512] for i in range(4)]
    ost = [S.xn[0:64, 4608 + i * 512:4608 + (i + 1) * 512] for i in range(2)]
    iotar = S.sb_at("a0_iotar", [1, 512], 0, F32)
    kpcs = S.sb_at("a0_kpcs", [128, 64], 2048, F32)
    qposr = S.sb_at("a0_qposr", [1, 512], 2304, F32)
    pS = [S.p512[0], S.p512[1]]
    qpbc = S.p512[2]
    pO = [S.pbig[0][0:65, :], S.pbig[1][0:65, :], S.pbig[2][0:65, :],
          S.p512[3][0:65, :]]
    bcb = [S.xr[1][:, 0:512], S.xw[0][:, 0:512]]

    g_io = st("gpsimd", lambda S, e: e.iota(
        iotar[:], pattern=[[1, 512]], base=0, channel_multiplier=0,
        allow_small_or_imprecise_dtypes=True))
    g_kp = st("gpsimd", lambda S, e: e.iota(
        kpcs[:], pattern=[[128, 64]], base=0, channel_multiplier=1,
        allow_small_or_imprecise_dtypes=True))
    lastq = [None] * 4
    lastk = [None, None]
    lastv = [None, None]
    lastex = [None, None]
    lastPb = [None] * 4
    lastnorm = [None] * 4
    lastbc = [None, None]
    last_mk = None
    last_qbc = None
    rt_r3 = [None] * 4
    last_r2 = None
    last_store = [None, None]
    rts = 0
    for qb in range(2):
        qp = st("vector", lambda S, e, qb=qb: e.tensor_single_scalar(
            qposr[:], iotar[:], S.pbr[0:1, 4 * qb:4 * qb + 1], ALU.add),
            deps=[g_io, last_qbc])
        qbc = st("tensor", lambda S, e: e.matmul(
            qpbc[:], S.ones_r32[:], qposr[:], start=True, stop=True),
            deps=[qp, last_mk])
        last_qbc = qbc
        for g in range(NKV0):
            qld = [None] * 4
            for i in range(4):
                h = 4 * g + i
                qld[i] = st("sync", lambda S, e, h=h, i=i, qb=qb: e.dma_start(
                    out=qt[i][:], in_=S.qfd[h // 2, (h % 2) * 64:(h % 2) * 64 + 64,
                                            qb * 512:qb * 512 + 512]),
                    deps=[lastq[i]], dma=True)
            pvq = [None] * 4
            for kt in range(KT0[qb]):
                r, slot = blk_owner(kt // 4)
                sl = kt % 2
                kld = st("sync", lambda S, e, r=r, g=g, slot=slot, kt=kt, sl=sl:
                         e.dma_start(out=kb[sl][:],
                                     in_=S.k0g[r, g, :,
                                               slot * 512 + (kt % 4) * 128:
                                               slot * 512 + (kt % 4) * 128 + 128]),
                         deps=[lastk[sl]], dma=True)
                vld = st("sync", lambda S, e, r=r, g=g, slot=slot, kt=kt, sl=sl:
                         e.dma_start(out=vb[sl][:],
                                     in_=S.v0g[r, slot * 512 + (kt % 4) * 128:
                                               slot * 512 + (kt % 4) * 128 + 128,
                                               65 * g:65 * g + 65]),
                         deps=[lastv[sl]], dma=True)
                for i in range(4):
                    psl = (4 * kt + i) % 2
                    ss = st("tensor", lambda S, e, sl=sl, i=i, psl=psl: e.matmul(
                        pS[psl][:], kb[sl][:], qt[i][:], start=True, stop=True),
                        deps=[kld, qld[i], lastex[psl]])
                    lastk[sl] = ss
                    lastq[i] = ss
                    ex = st("scalar", lambda S, e, i=i, psl=psl: e.activation(
                        out=Pb[i][:], in_=pS[psl][:], func=AF.Exp,
                        bias=S.nktc[:, 0:1]), deps=[ss, lastPb[i]])
                    lastex[psl] = ex
                    mk = st("vector", lambda S, e, i=i, kt=kt:
                            e.scalar_tensor_tensor(
                                out=Pb[i][:], in0=qpbc[:],
                                scalar=kpcs[:, kt:kt + 1], in1=Pb[i][:],
                                op0=ALU.is_ge, op1=ALU.mult),
                            deps=[ex, qbc, g_kp])
                    last_mk = mk
                    pde = [vld, mk]
                    if kt == 0:
                        pde.append(lastnorm[i])
                    pv = st("tensor", lambda S, e, sl=sl, i=i, kt=kt, qb=qb:
                            e.matmul(pO[i], vb[sl][:], Pb[i][:],
                                     start=(kt == 0), stop=(kt == KT0[qb] - 1)),
                            deps=pde)
                    lastv[sl] = pv
                    lastPb[i] = pv
                    pvq[i] = pv
            for i in range(4):
                h = 4 * g + i
                nsl = i % 2
                r1 = st("vector", lambda S, e, i=i: e.reciprocal(
                    out=S.xr[0][64:65, :], in_=pO[i][64:65, :]),
                    deps=[pvq[i], last_r2])
                slo = rts % 4
                rts += 1
                r2 = st("sync", lambda S, e, slo=slo: e.dma_start(
                    out=S.rt[slo:slo + 1, :], in_=S.xr[0][64:65, :]),
                    deps=[r1, rt_r3[slo]], dma=True)
                last_r2 = r2
                r3 = st("sync", lambda S, e, slo=slo, nsl=nsl: e.dma_start(
                    out=bcb[nsl][:],
                    in_=S.rt[slo:slo + 1, :].broadcast_to([128, 512])),
                    deps=[r2, lastbc[nsl]], dma=True)
                rt_r3[slo] = r3
                r4 = st("vector", lambda S, e, i=i, nsl=nsl: e.tensor_tensor(
                    out=ost[nsl][:], in0=pO[i][0:64, :], in1=bcb[nsl][0:64, :],
                    op=ALU.mult), deps=[r3, last_store[nsl]])
                so = st("sync", lambda S, e, h=h, qb=qb, nsl=nsl: e.dma_start(
                    out=S.ao0[h // 2, (h % 2) * 64:(h % 2) * 64 + 64,
                              qb * 512:qb * 512 + 512], in_=ost[nsl][:]),
                    deps=[r4], dma=True)
                last_store[nsl] = so
                lastnorm[i] = r4
                lastbc[nsl] = r4
    return sc


def make_v1build(S):
    sc = Script(S, "v1b")
    st = sc.step
    vbuf = [S.xn[:, i * 128:(i + 1) * 128] for i in range(2)]
    vstg = [S.xn[:, 256 + i * 129:256 + (i + 1) * 129] for i in range(2)]
    psV = S.pbf[:, 0:128]
    prev_ev = {}
    prev_st = {}
    for oc in range(8):
        for t in range(8):
            j = oc * 8 + t
            sl = j % 2
            ld = st("sync", lambda S, e, oc=oc, t=t, sl=sl: e.dma_start(
                out=vbuf[sl][:], in_=S.v1f[oc, :, t * 128:(t + 1) * 128]),
                deps=[prev_ev.get(sl)], dma=True)
            T = st("tensor", lambda S, e, sl=sl: e.transpose(
                psV, vbuf[sl][:], S.identb[:]), deps=[ld, prev_ev.get("ps")])
            c1 = st("vector", lambda S, e, sl=sl: e.tensor_copy(
                vstg[sl][:, 0:64], psV[:, 0:64]), deps=[T, prev_st.get(sl)])
            st("vector", lambda S, e, sl=sl: e.memset(vstg[sl][:, 64:65], 1.0))
            c3 = st("vector", lambda S, e, sl=sl: e.tensor_copy(
                vstg[sl][:, 65:129], psV[:, 64:128]))
            prev_ev[sl] = c3
            prev_ev["ps"] = c3
            prev_st[sl] = st("sync", lambda S, e, oc=oc, t=t, sl=sl: e.dma_start(
                out=S.v1c[t // 4, (t % 4) * 128:(t % 4) * 128 + 128,
                          129 * oc:129 * oc + 129],
                in_=vstg[sl][:]), deps=[c3], dma=True)
    return sc


def make_cc2(S):
    sc = Script(S, "cc2")
    st = sc.step
    items = [("q", h, hf) for h in range(32) for hf in range(2)] +             [("k", g, hf) for g in range(8) for hf in range(2)]
    xsl = [S.xn[:, i * 512:(i + 1) * 512] for i in range(2)]
    psr = [S.pbig[0][0:1, :], S.pbig[1][0:1, :]]
    mz = st("vector", lambda S, e: e.memset(S.mx48[:], 1e-30))
    lastred = [None, None]
    lastsq = [None, None]
    lastmm = [None, None]
    for i, (kind, idx, hf) in enumerate(items):
        sl = i % 2
        if kind == "q":
            src = S.q1d[idx, :, hf * 512:hf * 512 + 512]
            col = idx
        else:
            src = S.k1c[hf, idx]
            col = 32 + idx
        ld = st("sync", lambda S, e, src=src, sl=sl: e.dma_start(
            out=xsl[sl], in_=src), deps=[lastsq[sl], lastmm[sl]], dma=True)
        sq = st("vector", lambda S, e, sl=sl: e.tensor_tensor(
            out=xsl[sl], in0=xsl[sl], in1=xsl[sl], op=ALU.mult), deps=[ld])
        lastsq[sl] = sq
        mm = st("tensor", lambda S, e, sl=sl: e.matmul(
            psr[sl], S.ones_c[:], xsl[sl], start=True, stop=True),
            deps=[sq, ld, lastred[sl]])
        lastmm[sl] = mm
        rd = st("vector", lambda S, e, sl=sl: e.reduce_max(
            S.l1mtmp[:], psr[sl], axis=AX.X), deps=[mm])
        lastred[sl] = st("vector", lambda S, e, col=col: e.tensor_tensor(
            out=S.mx48[:, col:col + 1], in0=S.mx48[:, col:col + 1],
            in1=S.l1mtmp[:], op=ALU.max), deps=[mz])
    c2 = lastred[0]
    s1 = st("scalar", lambda S, e: e.activation(
        out=S.mx48[0:1, 0:48], in_=S.mx48[0:1, 0:48], func=AF.Sqrt),
        deps=[lastred[0], lastred[1]])
    d1 = st("sync", lambda S, e: e.dma_start(
        out=S.mx1c[0:1, 0:8], in_=S.mx48[0:1, 32:40]), deps=[s1], dma=True)
    cg = st("gpsimd", lambda S, e: e.collective_compute(
        "AllGather", ALU.bypass, replica_groups=[list(range(8))],
        ins=[S.mx1c[:]], outs=[S.mx1g[:]]), deps=[d1], coll=True)
    kg = st("gpsimd", lambda S, e: e.collective_compute(
        "AllGather", ALU.bypass, replica_groups=[list(range(8))],
        ins=[S.k1c[:]], outs=[S.k1g[:]]), coll=True)
    vg = st("gpsimd", lambda S, e: e.collective_compute(
        "AllGather", ALU.bypass, replica_groups=[list(range(8))],
        ins=[S.v1c[:]], outs=[S.v1g[:]]), coll=True)
    for qb in range(2):
        def _kcp(S, e, qb=qb):
            off = e.value_load(S.a1pt[0:1, 2 + qb:3 + qb])
            return e.dma_start(
                out=S.k1p.rearrange("s g p u -> s (g p u)")[qb:qb + 1, :],
                in_=S.k1g.rearrange("r g p u -> r (g p u)")[bass.ds(off, 1), :])
        st("sync", _kcp, deps=[kg], dma=True)

        def _vcp(S, e, qb=qb):
            off = e.value_load(S.a1pt[0:1, 2 + qb:3 + qb])
            return e.dma_start(
                out=S.v1p.rearrange("s u f -> s (u f)")[qb:qb + 1, :],
                in_=S.v1g.rearrange("r u f -> r (u f)")[bass.ds(off, 1), :])
        st("sync", _vcp, deps=[vg], dma=True)
    l1 = st("sync", lambda S, e: e.dma_start(
        out=S.xr[0][0:1, 0:64],
        in_=S.mx1g.rearrange("r c -> (r c)").rearrange("(o f) -> o f", o=1)),
        deps=[cg], dma=True)
    gt = l1
    for r in range(1, 8):
        gt = st("vector", lambda S, e, r=r: e.tensor_tensor(
            out=S.xr[0][0:1, 0:8], in0=S.xr[0][0:1, 0:8],
            in1=S.xr[0][0:1, 8 * r:8 * r + 8], op=ALU.max),
            deps=[gt] if r == 1 else [])
    s2 = st("sync", lambda S, e: e.dma_start(
        out=S.rt[6:7, 0:8], in_=S.xr[0][0:1, 0:8]), deps=[gt], dma=True)
    kxs = []
    for i in range(4):
        kxs.append(st("sync", lambda S, e, i=i: e.dma_start(
            out=S.xr[1][0:1, 64:96].rearrange(
                "o (g i2) -> o g i2", g=8)[:, :, i:i + 1]
            .rearrange("o g i2 -> o (g i2)"),
            in_=S.rt[6:7, 0:8], allow_slow_non_contiguous=True),
            deps=[s2], dma=True))
    cr = st("vector", lambda S, e: e.tensor_tensor(
        out=S.xr[0][0:1, 64:96], in0=S.mx48[0:1, 0:32],
        in1=S.xr[1][0:1, 64:96], op=ALU.mult), deps=kxs + [s1])
    cr2 = st("vector", lambda S, e: e.tensor_scalar(
        out=S.xr[0][0:1, 64:96], in0=S.xr[0][0:1, 64:96],
        scalar1=-1.02 * SC1, scalar2=None, op0=ALU.mult))
    s3 = st("sync", lambda S, e: e.dma_start(
        out=S.rt[7:8, 0:32], in_=S.xr[0][0:1, 64:96]), deps=[cr2], dma=True)
    st("gpsimd", lambda S, e: e.dma_start(
        out=S.c1neg[:, 0:32], in_=S.rt[7:8, 0:32].broadcast_to([128, 32])),
        deps=[s3], dma=True)
    return sc



def make_attn1(S):
    sc = Script(S, "at1")
    st = sc.step
    qt = S.xn[:, 0:512]
    kb = [S.xn[:, 512 + i * 128:512 + (i + 1) * 128] for i in range(2)]
    va = [S.xn[:, 768 + i * 65:768 + (i + 1) * 65] for i in range(2)]
    vbb = [S.xn[:, 898 + i * 64:898 + (i + 1) * 64] for i in range(2)]
    Pb = [S.xn[:, 1026 + i * 512:1026 + (i + 1) * 512] for i in range(2)]
    osA = S.xn[0:64, 2050:2562]
    osB = S.xn[0:64, 2562:3074]
    iotar = S.sb_at("a1_iotar", [1, 512], 0, F32)
    kpcs = S.sb_at("a1_kpcs", [128, 8], 2048, F32)
    qposr = S.sb_at("a1_qposr", [1, 512], 2304, F32)
    pS = [S.p512[0], S.p512[1]]
    qpbc = S.p512[2]
    qpbc2 = S.p512[3]
    pOA = S.pbig[0][0:65, :]
    pOB = S.pbig[1][0:64, :]
    g_io = st("gpsimd", lambda S, e: e.iota(
        iotar[:], pattern=[[1, 512]], base=0, channel_multiplier=0,
        allow_small_or_imprecise_dtypes=True))
    lastq = None
    lastk = [None, None]
    lastva = [None, None]
    lastvb = [None, None]
    lastP = [None, None]
    lastex = [None, None]
    last_mk = None
    last_qbc2 = None
    rt_r3 = [None] * 4
    last_r2 = None
    lastoA = lastoB = None
    normA = normB = None
    prev_r4 = None
    rts = 0
    for qb in range(2):
        kpt = []
        for rk in range(8):
            k1tok = st("vector", lambda S, e, rk=rk, qb=qb: e.tensor_scalar(
                out=kpcs[:, rk:rk + 1], in0=S.posc[:, 4 * qb:4 * qb + 1],
                scalar1=float(128 * rk - 512), scalar2=None, op0=ALU.add))
            if rk < 4:
                k1tok = st("vector", lambda S, e, rk=rk, qb=qb: e.tensor_single_scalar(
                    kpcs[:, rk:rk + 1], kpcs[:, rk:rk + 1],
                    S.pbr[:, 8 + qb:9 + qb], ALU.add))
            kpt.append(k1tok)
        qp = st("vector", lambda S, e, qb=qb: e.tensor_single_scalar(
            qposr[:], iotar[:], S.pbr[0:1, 4 * qb:4 * qb + 1], ALU.add),
            deps=[g_io, last_qbc2])
        qbc = st("tensor", lambda S, e: e.matmul(
            qpbc[:], S.ones_r32[:], qposr[:], start=True, stop=True),
            deps=[qp, last_mk])
        qp2 = st("vector", lambda S, e: e.tensor_scalar(
            out=qposr[:], in0=qposr[:], scalar1=-512.0, scalar2=None,
            op0=ALU.add), deps=[qbc])
        qbc2 = st("tensor", lambda S, e: e.matmul(
            qpbc2[:], S.ones_r32[:], qposr[:], start=True, stop=True),
            deps=[qp2, last_mk])
        last_qbc2 = qbc2
        for g in range(NKV1):
            for h2 in range(4):
                h = 4 * g + h2
                qld = st("sync", lambda S, e, h=h, qb=qb: e.dma_start(
                    out=qt[:], in_=S.q1d[h, :, qb * 512:(qb + 1) * 512]),
                    deps=[lastq], dma=True)
                for rk in range(8):
                    sl = rk % 2
                    mp = rk * 128 if rk < 4 else (rk - 4) * 128
                    if rk >= 4:
                        kld = st("sync", lambda S, e, g=g, qb=qb, mp=mp, sl=sl:
                                 e.dma_start(out=kb[sl][:],
                                             in_=S.k1c[qb, g, :, mp:mp + 128]),
                                 deps=[lastk[sl]], dma=True)
                        vld = st("sync", lambda S, e, g=g, qb=qb, mp=mp, sl=sl:
                                 e.dma_start(out=va[sl][:],
                                             in_=S.v1c[qb, mp:mp + 128,
                                                       129 * g:129 * g + 65]),
                                 deps=[lastva[sl]], dma=True)
                        vld2 = st("sync", lambda S, e, g=g, qb=qb, mp=mp, sl=sl:
                                  e.dma_start(out=vbb[sl][:],
                                              in_=S.v1c[qb, mp:mp + 128,
                                                        129 * g + 65:129 * g + 129]),
                                  deps=[lastvb[sl]], dma=True)
                    else:
                        kld = st("sync", lambda S, e, g=g, qb=qb, mp=mp, sl=sl:
                                 e.dma_start(out=kb[sl][:],
                                             in_=S.k1p[qb, g, :, mp:mp + 128]),
                                 deps=[lastk[sl]], dma=True)
                        vld = st("sync", lambda S, e, g=g, qb=qb, mp=mp, sl=sl:
                                 e.dma_start(out=va[sl][:],
                                             in_=S.v1p[qb, mp:mp + 128,
                                                       129 * g:129 * g + 65]),
                                 deps=[lastva[sl]], dma=True)
                        vld2 = st("sync", lambda S, e, g=g, qb=qb, mp=mp, sl=sl:
                                  e.dma_start(out=vbb[sl][:],
                                              in_=S.v1p[qb, mp:mp + 128,
                                                        129 * g + 65:129 * g + 129]),
                                  deps=[lastvb[sl]], dma=True)
                    ss = st("tensor", lambda S, e, sl=sl: e.matmul(
                        pS[sl][:], kb[sl][:], qt[:], start=True, stop=True),
                        deps=[kld, qld, lastex[sl]])
                    lastk[sl] = ss
                    lastq = ss
                    ex = st("scalar", lambda S, e, sl=sl, h=h: e.activation(
                        out=Pb[sl][:], in_=pS[sl][:], func=AF.Exp, scale=SC1,
                        bias=S.c1neg[:, h:h + 1]), deps=[ss, lastP[sl]])
                    lastex[sl] = ex
                    st("vector", lambda S, e, sl=sl, rk=rk: e.scalar_tensor_tensor(
                        out=Pb[sl][:], in0=qpbc[:], scalar=kpcs[:, rk:rk + 1],
                        in1=Pb[sl][:], op0=ALU.is_ge, op1=ALU.mult),
                        deps=[ex, qbc, kpt[rk]])
                    mk2 = st("vector", lambda S, e, sl=sl, rk=rk: e.scalar_tensor_tensor(
                        out=Pb[sl][:], in0=qpbc2[:], scalar=kpcs[:, rk:rk + 1],
                        in1=Pb[sl][:], op0=ALU.is_lt, op1=ALU.mult),
                        deps=[qbc2])
                    last_mk = mk2
                    pdeA = [vld, mk2]
                    pdeB = [vld2]
                    if rk == 0:
                        pdeA += [normA, normB]
                        pdeB.append(normB)
                    pvA = st("tensor", lambda S, e, sl=sl, rk=rk: e.matmul(
                        pOA, va[sl][:], Pb[sl][:],
                        start=(rk == 0), stop=(rk == 7)), deps=pdeA)
                    lastva[sl] = pvA
                    pvB = st("tensor", lambda S, e, sl=sl, rk=rk: e.matmul(
                        pOB, vbb[sl][:], Pb[sl][:],
                        start=(rk == 0), stop=(rk == 7)), deps=pdeB)
                    lastvb[sl] = pvB
                    lastP[sl] = pvB
                r1 = st("vector", lambda S, e: e.reciprocal(
                    out=S.xr[0][64:65, :], in_=pOA[64:65, :]),
                    deps=[lastP[7 % 2], last_r2])
                slo = rts % 4
                rts += 1
                r2 = st("sync", lambda S, e, slo=slo: e.dma_start(
                    out=S.rt[slo:slo + 1, :], in_=S.xr[0][64:65, :]),
                    deps=[r1, rt_r3[slo]], dma=True)
                last_r2 = r2
                r3 = st("gpsimd", lambda S, e, slo=slo: e.dma_start(
                    out=S.xr[1][:], in_=S.rt[slo:slo + 1, :].broadcast_to([128, 512])),
                    deps=[r2, prev_r4], dma=True)
                rt_r3[slo] = r3
                r4a = st("vector", lambda S, e: e.tensor_tensor(
                    out=osA[:], in0=pOA[0:64, :], in1=S.xr[1][0:64, :],
                    op=ALU.mult), deps=[r3, lastoA])
                r4b = st("vector", lambda S, e: e.tensor_tensor(
                    out=osB[:], in0=pOB[0:64, :], in1=S.xr[1][0:64, :],
                    op=ALU.mult), deps=[lastoB])
                normA, normB = r4a, r4b
                prev_r4 = r4b
                lastoA = st("sync", lambda S, e, h=h, qb=qb: e.dma_start(
                    out=S.ao1[h, 0:64, qb * 512:(qb + 1) * 512], in_=osA[:]),
                    deps=[r4a], dma=True)
                lastoB = st("sync", lambda S, e, h=h, qb=qb: e.dma_start(
                    out=S.ao1[h, 64:128, qb * 512:(qb + 1) * 512], in_=osB[:]),
                    deps=[r4b], dma=True)
    return sc



def make_dequant(S):
    sc = Script(S, "dq")
    st = sc.step
    NT = 256
    OGT = 1024
    crep = lambda cb, sl: S.xn[:, (2 * sl + cb) * OGT:(2 * sl + cb + 1) * OGT]
    oht = lambda q, sl: S.xn[:, (4 + 4 * sl + q) * OGT:(4 + 4 * sl + q + 1) * OGT]
    stg = [S.xn[:, 12 * OGT + i * 512:12 * OGT + (i + 1) * 512] for i in range(2)]
    cbb = S.sb_at("dq_cbb", [128, 32], 12288, BF16)
    iot2 = S.sb_at("dq_iot2", [128, 2], 12352, F32)
    ps = [S.p512[0], S.p512[1]]
    st("gpsimd", lambda S, e: e.iota(iot2[:, 0:1], pattern=[[0, 1]], base=0,
                                     channel_multiplier=1,
                                     allow_small_or_imprecise_dtypes=True))
    g_io = st("gpsimd", lambda S, e: e.iota(iot2[:, 1:2], pattern=[[0, 1]], base=128,
                                            channel_multiplier=1,
                                            allow_small_or_imprecise_dtypes=True))
    cbl = []
    for cb in range(2):
        for ec in range(2):
            cbl.append(st("gpsimd", lambda S, e, cb=cb, ec=ec: e.dma_start(
                out=cbb[:, (2 * cb + ec) * 8:(2 * cb + ec) * 8 + 8],
                in_=S.cbflat[256 * cb + 128 * ec:256 * cb + 128 * ec + 128, :]),
                dma=True))
    gcast = st("gpsimd", lambda S, e: e.dma_start(
        out=S.codesb[:], in_=S.codes[:]), dma=True)
    prev_mm = {}
    prev_store = {}
    prev_ohs = {}
    for t in range(NT):
        sl = t % 2
        reps, ohs = [], {}
        for cb in range(2):
            reps.append(st(("sync", "scalar")[cb], lambda S, e, cb=cb, t=t, sl=sl:
                           e.dma_start(out=crep(cb, sl),
                                       in_=S.codesb[cb:cb + 1, t * OGT:(t + 1) * OGT]
                                       .broadcast_to([128, OGT])),
                           deps=[prev_mm.get(t - 2), prev_ohs.get((sl, cb)),
                                 gcast], dma=True))
        for cb in range(2):
            for ec in range(2):
                ohs[(cb, ec)] = st("vector", lambda S, e, cb=cb, ec=ec, sl=sl:
                                   e.tensor_single_scalar(
                                       oht(2 * cb + ec, sl), crep(cb, sl),
                                       iot2[:, ec:ec + 1], ALU.is_equal),
                                   deps=[reps[cb], prev_mm.get(t - 2), g_io])
                prev_ohs[(sl, cb)] = ohs[(cb, ec)]
        for srt in range(2):
            gi = 2 * t + srt
            pss = ps[gi % 2]
            mm = None
            for ci, (cb, ec) in enumerate([(0, 0), (0, 1), (1, 0), (1, 1)]):
                mm = st("tensor", lambda S, e, cb=cb, ec=ec, srt=srt, pss=pss,
                        ci=ci, sl=sl:
                        e.matmul(pss[0:8, :],
                                 cbb[:, (2 * cb + ec) * 8:(2 * cb + ec) * 8 + 8],
                                 oht(2 * cb + ec, sl)[:, srt * 512:(srt + 1) * 512],
                                 start=(ci == 0), stop=(ci == 3)),
                        deps=[ohs[(cb, ec)], prev_store.get(gi - 2)] + cbl)
            prev_mm[t] = mm
            ev = st("vector", lambda S, e, gi=gi, pss=pss: e.tensor_copy(
                stg[gi % 2][0:8, :], pss[0:8, :]),
                deps=[mm, prev_store.get(gi - 2)])
            prev_store[gi] = st("sync", lambda S, e, gi=gi: e.dma_start(
                out=S.wqc[8 * gi:8 * gi + 8, :],
                in_=stg[gi % 2][0:8, :]), deps=[ev], dma=True)
    st("gpsimd", lambda S, e: e.collective_compute(
        "AllGather", ALU.bypass, replica_groups=[list(range(8))],
        ins=[S.wqc[:]], outs=[S.wqg[:]]),
        deps=[prev_store[2 * NT - 1 - k] for k in range(8)], coll=True)
    return sc


def make_outstore(S):
    sc = Script(S, "out")
    st = sc.step
    toks = []
    for c in range(NRES):
        toks.append(st("sync", lambda S, e, c=c: e.dma_start(
            out=S.out[c * 128:(c + 1) * 128, :], in_=S.xs[:, c * TOK:(c + 1) * TOK]),
            dma=True))
    for c in range(NRES, NCH):
        toks.append(st("sync", lambda S, e, c=c: e.dma_start(
            out=S.out[c * 128:(c + 1) * 128, :], in_=S.xpark[c - NRES]),
            dma=True))
    st("sync", lambda S, e: e.nop(), deps=toks)
    return sc


def _add_casts(sc, casts):
    for src, dst, n in casts:
        for oc in range(n):
            sc.step("gpsimd", lambda S, e, src=src, dst=dst, oc=oc: e.dma_start(
                out=dst[oc], in_=src[oc]), dma=True)
    return sc


def build_phases(S, upto=None, skip=()):
    wq_slabs = [S.wqg[oc // 4, :, (oc % 4) * 128:(oc % 4) * 128 + 128]
                .rearrange("(c p) o -> p c o", p=128) for oc in range(32)]
    mk_cv1 = lambda: make_conv(S, [
        (lambda oc: S.qc1[oc], 8, 1, lambda S_, c, oc: S_.q0[:, oc, :], q1dst,
         lambda S_, oc: S_.cbq1[:, oc:oc + 1], lambda b: 516 * b),
        (lambda oc: S.kc1[oc], 2, 1, lambda S_, c, oc: S_.k0[:, oc, :], k1dst,
         lambda S_, oc: S_.cbk1[:, oc:oc + 1], lambda b: 516 * b),
    ], pre2=True, tag="cv1")
    mk_cv2 = lambda: make_conv(S, [
        (lambda oc: S.qc2T[oc], 8, 8, lambda S_, c, oc: S_.q1[:, c, :], q2dst,
         lambda S_, oc: S_.cbq2[:, oc:oc + 1], lambda b: 514 * b),
        (lambda oc: S.kc2T[oc], 2, 2, lambda S_, c, oc: S_.k1[:, c, :], k2dst,
         lambda S_, oc: S_.cbk2[:, oc:oc + 1], lambda b: 514 * b),
    ], pre2=False, tag="cv2")
    phases = [
        ("p0", lambda: make_p0(S)),
        ("dq", lambda: make_dequant(S)),
        ("lns1", lambda: make_lnstats(S, "lns1", "ln")),
        ("qkv", lambda: make_proj(S, [([S.qkvT[oc] for oc in range(12)], 12,
                                       evac_qkv())], tag="qkv")),
        ("halo", lambda: make_halocc(S)),
        ("cv1", mk_cv1),
        ("cv2", mk_cv2),
        ("mf", lambda: make_meanfix(S)),
        ("pp", lambda: make_postproc(S)),
        ("cc1", lambda: make_cc1(S)),
        ("at0", lambda: make_attn0(S)),
        ("wo0", lambda: make_ao_proj(S, S.woT, 32, 8, S.ao0, tag="wo0")),
        ("lns2", lambda: make_lnstats(S, "lns2", "ln+rms")),
        ("ffn0", lambda: make_proj(
            S, [([S.ffn0T[oc] for oc in range(32)], 32,
                 evac_residual(scale=lambda S_, oc: S_.gcols[:, oc:oc + 1]))],
            tag="ffn0", post_nw=True)),
        ("lns3", lambda: make_lnstats(S, "lns3", "ln")),
        ("l1p", lambda: make_l1proj(S)),
        ("v1b", lambda: make_v1build(S)),
        ("cc2", lambda: make_cc2(S)),
        ("at1", lambda: make_attn1(S)),
        ("wo1", lambda: make_ao_proj(S, S.l1oT, 32, 32, S.ao1, tag="wo1")),
        ("lns4", lambda: make_lnstats(S, "lns4", "ln")),
        ("ffn1", lambda: make_proj(S, [(wq_slabs, 32,
                                        evac_residual(scale=lambda S_, oc:
                                                      S_.scol[:, oc:oc + 1],
                                                      bias=lambda S_, oc:
                                                      S_.bcol[:, oc:oc + 1]))],
                                   tag="ffn1", weng="scalar")),
        ("out", lambda: make_outstore(S)),
    ]
    out = []
    for name, mk in phases:
        if name in skip:
            continue
        sc = mk()
        sc.name = name
        sc.snap_drain()
        out.append(sc)
        if upto is not None and name == upto:
            break
    return out



def _slabs(wT, n_oc, kdim):
    nc_ = kdim // 128
    return np.ascontiguousarray(
        wT.reshape(nc_, 128, n_oc, 128).transpose(2, 1, 0, 3)
        .reshape(n_oc, 128, nc_ * 128).astype(np.float32))


def _pack_conv1(w, n_oc):
    out = np.zeros((n_oc, 128, 3, 128), dtype=np.float32)
    for oc in range(n_oc):
        for o2 in range(128):
            och = 128 * oc + o2
            hh = o2 // 64
            for tau in range(3):
                out[oc, hh * 64:(hh + 1) * 64, tau, o2] = w[och, :, tau]
    return np.ascontiguousarray(out.reshape(n_oc, 128, 384))


def _pack_conv2(w, n_oc, kch):
    out = np.zeros((n_oc, 128, 3, kch, 128), dtype=np.float32)
    for oc in range(n_oc):
        blk = w[oc * 128:(oc + 1) * 128]
        for tau in range(3):
            for ic in range(kch):
                out[oc, :, tau, ic, :] = blk[:, ic * 128:(ic + 1) * 128, tau].T
    return np.ascontiguousarray(out.reshape(n_oc, 128, 3 * kch * 128))


def _cols(v):
    return np.ascontiguousarray(v.reshape(32, 128).T.astype(np.float32))


def make_in_maps(inp):
    xt = np.ascontiguousarray(inp["L0_attn_x"][0].T.astype(np.float32))
    wqkvT = _slabs(np.ascontiguousarray(inp["L0_attn_w_qkv"].T), 12, D)
    qc1 = _pack_conv1(inp["L0_attn_qc1_w"], 8)
    qc2 = _pack_conv2(inp["L0_attn_qc2_w"], 8, 8)
    kc1 = _pack_conv1(inp["L0_attn_kc1_w"], 2)
    kc2 = _pack_conv2(inp["L0_attn_kc2_w"], 2, 2)
    cbq1 = np.ascontiguousarray(inp["L0_attn_qc1_b"].reshape(8, 128).T.astype(np.float32))
    cbq2 = np.ascontiguousarray(inp["L0_attn_qc2_b"].reshape(8, 128).T.astype(np.float32))
    cbk1 = np.ascontiguousarray(inp["L0_attn_kc1_b"].reshape(2, 128).T.astype(np.float32))
    cbk2 = np.ascontiguousarray(inp["L0_attn_kc2_b"].reshape(2, 128).T.astype(np.float32))
    ktv = inp["L0_attn_key_temp"].reshape(1, 1).astype(np.float32)
    woT = _slabs(np.ascontiguousarray(inp["L0_attn_w_o"].T), 32, 1024)
    ffn0T = _slabs(np.ascontiguousarray(inp["L0_ffn_w_q"].T), 32, D)
    nwc = _cols(inp["L0_ffn_norm_weight"])
    gam = _cols(inp["L0_ffn_gamma"])
    l1qT = _slabs(np.ascontiguousarray(inp["L1_attn_wq"].T), 32, D)
    l1kT = _slabs(np.ascontiguousarray(inp["L1_attn_wk"].T), 8, D)
    l1vT = _slabs(np.ascontiguousarray(inp["L1_attn_wv"].T), 8, D)
    l1oT = _slabs(np.ascontiguousarray(inp["L1_attn_wo"].T), 32, D)
    cbf = np.ascontiguousarray(
        inp["L1_ffn_codebooks"].reshape(512, 8).astype(np.float32))
    scl = _cols(inp["L1_ffn_scales"].reshape(4096))
    fb = _cols(inp["L1_ffn_bias"])

    in_maps = []
    for core in range(8):
        bA, bB = core, 15 - core
        xc = np.ascontiguousarray(np.concatenate(
            [xt[:, bA * TB:(bA + 1) * TB], xt[:, bB * TB:(bB + 1) * TB]], axis=1))
        codes_sl = inp["L1_ffn_codes"][core * 512:(core + 1) * 512]
        codes_rows = np.ascontiguousarray(
            codes_sl.transpose(2, 1, 0).reshape(2, 512 * 512).astype(np.int32))
        pb = np.zeros((1, 16), dtype=np.float32)
        for t in range(8):
            pb[0, t] = (bA if t < 4 else bB) * TB + (t % 4) * 128
        for qb, blk in enumerate((bA, bB)):
            if blk == 0:
                pb[0, 8 + qb] = 1e9
        prm = np.zeros((1, 16), dtype=np.int32)
        for qb, blk in enumerate((bA, bB)):
            if blk > 0:
                r, sl = blk_owner(blk - 1)
                prm[0, qb] = (r * 2 + sl) * 12
                prm[0, 2 + qb] = r * 2 + sl
                for mm4 in range(4):
                    prm[0, 4 + 4 * qb + mm4] = (r * 2 + sl) * 512 + mm4 * 128
        hz = np.ones((128, 2), dtype=np.float32)
        if bA == 0:
            hz[:, 0] = 0.0
        in_maps.append({
            "x": xc, "qkvT": wqkvT, "qc1": qc1, "qc2T": qc2, "kc1": kc1,
            "kc2T": kc2, "cb_q1": cbq1, "cb_q2": cbq2, "cb_k1": cbk1,
            "cb_k2": cbk2, "key_temp": ktv, "woT": woT, "ffn0T": ffn0T,
            "nwc": nwc, "gamma": gam, "l1qT": l1qT, "l1kT": l1kT, "l1vT": l1vT,
            "l1oT": l1oT, "codes": codes_rows, "cbflat": cbf, "scales": scl,
            "fbias": fb, "pbase": pb, "prm": prm, "hz": hz,
        })
    return in_maps


_CACHE = {}


def kernel(**inputs):
    inp = {k: np.asarray(v) for k, v in inputs.items()}
    in_maps = make_in_maps(inp)
    if "nc" not in _CACHE:
        S = build_state(0)
        phases = build_phases(S)
        _CACHE["nc"] = finalize(S, phases)
    nc = _CACHE["nc"]
    res = run_bass_kernel_spmd(nc, in_maps, core_ids=list(range(8)))
    out = np.zeros((1, 8192, D), dtype=np.float32)
    for core in range(8):
        o = np.asarray(res.results[core]["out"])
        bA, bB = core, 15 - core
        out[0, bA * TB:(bA + 1) * TB, :] = o[:, 0:TB].T
        out[0, bB * TB:(bB + 1) * TB, :] = o[:, TB:2 * TB].T
    return out


# version 08 (final)
# speedup vs baseline: 1.0419x; 1.0419x over previous
import math
from contextlib import ExitStack
from types import SimpleNamespace

import numpy as np
import concourse.bass as bass
import concourse.mybir as mybir
from concourse.bass_utils import run_bass_kernel_spmd

F32 = mybir.dt.float32
BF16 = mybir.dt.bfloat16
I32 = mybir.dt.int32
AF = mybir.ActivationFunctionType
ALU = mybir.AluOpType
AX = mybir.AxisListType
PI = math.pi
TWO_PI = 2.0 * math.pi

D, NCH, TB, TOK = 4096, 32, 512, 1024
NRES = 24
NPARK = NCH - NRES
NH0, NKV0, HD0 = 16, 4, 64
NH1, NKV1, HD1 = 32, 8, 128
EPS_LN, EPS_RMS = 1e-5, 1e-6
OWNL = [0, 512]
SC1 = 1.0 / math.sqrt(128.0)
KT0 = [32, 64]

ENGS = ("sync", "gpsimd", "vector", "scalar", "tensor")
DPOOL = {"sync": 8, "gpsimd": 4, "scalar": 4}


def blk_owner(g):
    r = min(g, 15 - g)
    return r, (0 if g <= 7 else 1)


class Script:

    def __init__(self, S, tag):
        self.S, self.tag = S, tag
        self.prog = {n: [] for n in ENGS}
        self.ncoll = 0
        self.drain = None
        if getattr(S, "pool", None) is None:
            S.pool = {
                "c": {n: [S.sem(f"plc_{n}"), 0] for n in ENGS},
                "d": {n: [[S.sem(f"pld_{n}{k}") for k in range(DPOOL[n])],
                          [0] * DPOOL[n], 0] for n in DPOOL},
                "lastc": {n: None for n in ENGS},
                "colls": [],
            }
        self.pool = S.pool

    def snap_drain(self):
        d = []
        for n in DPOOL:
            dp = self.pool["d"][n]
            for p in range(len(dp[0])):
                if dp[1][p]:
                    d.append((dp[0][p], 16 * dp[1][p]))
        d += [(sem, 1) for sem in self.pool["colls"]]
        self.drain = d

    def step(self, eng, fn, deps=(), dma=False, coll=False, chain=None):
        deps = [t for t in deps if t is not None]
        if coll:
            sem = self.S.sem(f"{self.tag}_cl{self.ncoll}")
            self.pool["colls"].append(sem)
            self.ncoll += 1
            tok = ("s", sem, 1)
            self.prog[eng].append((fn, tuple(deps), tok, None, 1))
            return tok
        if dma:
            dp = self.pool["d"][eng]
            p = dp[2] % len(dp[0])
            dp[2] += 1
            dp[1][p] += 1
            u = dp[1][p]
            tok = ("d", eng, p, u)
            selfwait = ("d", eng, p, u - 1) if u > 1 else None
            self.prog[eng].append((fn, tuple(deps), tok, selfwait, 16))
            return tok
        cp = self.pool["c"][eng]
        cp[1] += 1
        tok = ("c", eng, cp[1])
        if chain is None:
            chain = True
        if chain and self.pool["lastc"][eng] is not None:
            deps = deps + [self.pool["lastc"][eng]]
        self.pool["lastc"][eng] = tok
        self.prog[eng].append((fn, tuple(deps), tok, None, 1))
        return tok

    def _resolve(self, t):
        if t[0] == "c":
            return self.pool["c"][t[1]][0], t[2]
        if t[0] == "d":
            return self.pool["d"][t[1]][0][t[2]], 16 * t[3]
        return t[1], t[2]

    def _emit(self, engname, e):
        S = self.S
        for fn, deps, tok, selfwait, inc in self.prog[engname]:
            fold = None
            waits = {}
            for t in deps:
                sem, val = self._resolve(t)
                if fold is None and t[0] == "c" and t[1] == engname:
                    fold = (sem, val)
                    continue
                k = id(sem)
                if k not in waits or waits[k][1] < val:
                    waits[k] = (sem, val)
            if selfwait is not None:
                sem, val = self._resolve(selfwait)
                k = id(sem)
                if k not in waits or waits[k][1] < val:
                    waits[k] = (sem, val)
            for sem, val in waits.values():
                e.wait_ge(sem, val)
            ins = fn(S, e)
            if ins is None:
                ins = e.nop()
            if fold is not None:
                ins._wait_ge(fold[0], fold[1])
            ins.then_inc(self._resolve(tok)[0], inc)
        if engname == "sync" and self.drain:
            for sem, val in self.drain:
                e.wait_ge(sem, val)


def build_state(core_id=0):
    nc = bass.Bass("TRN2", target_bir_lowering=False)
    S = SimpleNamespace(nc=nc, core=core_id, pool=None)

    di = lambda n, sh, dt=F32: nc.dram_tensor(n, sh, dt, kind="ExternalInput").ap()
    S.x = di("x", [D, TOK])
    S.qkvT = di("qkvT", [12, 128, D])
    S.qc1 = di("qc1", [8, 128, 384])
    S.qc2T = di("qc2T", [8, 128, 3072])
    S.kc1 = di("kc1", [2, 128, 384])
    S.kc2T = di("kc2T", [2, 128, 768])
    S.cb_q1 = di("cb_q1", [128, 8])
    S.cb_q2 = di("cb_q2", [128, 8])
    S.cb_k1 = di("cb_k1", [128, 2])
    S.cb_k2 = di("cb_k2", [128, 2])
    S.key_temp = di("key_temp", [1, 1])
    S.woT = di("woT", [32, 128, 1024])
    S.ffn0T = di("ffn0T", [32, 128, D])
    S.nwc = di("nwc", [128, 32])
    S.gamma = di("gamma", [128, 32])
    S.l1qT = di("l1qT", [32, 128, D])
    S.l1kT = di("l1kT", [8, 128, D])
    S.l1vT = di("l1vT", [8, 128, D])
    S.l1oT = di("l1oT", [32, 128, D])
    S.codes = di("codes", [2, 512 * 512], I32)
    S.cbflat = di("cbflat", [512, 8])
    S.scales = di("scales", [128, 32])
    S.fbias = di("fbias", [128, 32])
    S.pbase = di("pbase", [1, 16])
    S.prm = di("prm", [1, 16], I32)
    S.hz = di("hz", [128, 2])

    S.out = nc.dram_tensor("out", [D, TOK], F32, kind="ExternalOutput").ap()

    dd = lambda n, sh, dt=BF16: nc.dram_tensor(n, sh, dt).ap()
    ds_ = lambda n, sh, dt=BF16: nc.dram_tensor(n, sh, dt,
                                                addr_space="Shared").ap()
    S.xpark = dd("xpark", [NPARK, 128, TOK], F32)
    S.hqc = dd("hqc", [2, 12, 128, 4])
    S.hqg = ds_("hqg", [8, 2, 12, 128, 4])
    S.k0c = dd("k0c", [NKV0, 64, TOK])
    S.v0c = dd("v0c", [TOK, NKV0 * 65])
    S.k0g = ds_("k0g", [8, NKV0, 64, TOK])
    S.v0g = ds_("v0g", [8, TOK, NKV0 * 65])
    S.qfd = dd("qfd", [8, 128, TOK])
    S.q1d = dd("q1d", [NH1, 128, TOK])
    S.k1c = dd("k1c", [2, NKV1, 128, 512])
    S.v1c = dd("v1c", [2, 512, NKV1 * 129])
    S.v1f = dd("v1f", [8, 128, TOK])
    S.k1g = ds_("k1g", [16, NKV1, 128, 512])
    S.v1g = ds_("v1g", [16, 512, NKV1 * 129])
    S.k1p = dd("k1p", [2, NKV1, 128, 512])
    S.v1p = dd("v1p", [2, 512, NKV1 * 129])
    S.mx1c = dd("mx1c", [1, 8], F32)
    S.mx1g = ds_("mx1g", [8, 8], F32)
    S.ao0 = dd("ao0", [8, 128, TOK])
    S.ao1 = dd("ao1", [NCH, 128, TOK])
    S.wqc = dd("wqc", [D, 512])
    S.codesb = dd("codesb", [2, 512 * 512])
    S.ffn0b = dd("ffn0b", [32, 128, D])
    S.l1qb = dd("l1qb", [32, 128, D])
    S.l1kb = dd("l1kb", [8, 128, D])
    S.l1vb = dd("l1vb", [8, 128, D])
    S.l1ob = dd("l1ob", [32, 128, D])
    S.wqg = ds_("wqg", [8, D, 512])
    S.rt = dd("rt", [8, 512], F32)

    ctx = ExitStack()
    S.ctx = ctx
    S.sem = lambda name: ctx.enter_context(nc.semaphore(name))
    S.sb = lambda name, shape, dt=BF16: ctx.enter_context(nc.sbuf_tensor(name, shape, dt))
    S.psum = lambda name, shape, dt=F32: ctx.enter_context(nc.psum_tensor(name, shape, dt))

    S.xs = S.sb("xs", [128, NRES * TOK], F32)
    S.xw = [S.sb(f"xw{k}", [128, TOK], F32) for k in range(2)]
    S.xr = [S.sb(f"xr{k}", [128, 512], F32) for k in range(2)]
    S.xn = S.sb("xn", [128, NCH * 512], BF16)
    S.wb = [S.sb(f"wb{k}", [128, NCH * 128], BF16) for k in range(2)]
    S.arena = S.sb("arena", [128, 12384], BF16)
    S.ones_c = S.sb("ones_c", [128, 1], BF16)
    S.ones_cf = S.sb("ones_cf", [128, 1], F32)
    S.ones_r32 = S.sb("ones_r32", [1, 128], F32)
    S.identb = S.sb("identb", [128, 128], BF16)
    S.identf = S.sb("identf", [128, 128], F32)
    S.ktep = S.sb("ktep", [128, 1], F32)
    S.nktc = S.sb("nktc", [128, 1], F32)
    S.invfr = S.sb("invfr", [128, 128], F32)
    S.signr = S.sb("signr", [128, 128], F32)
    S.posc = S.sb("posc", [128, 8], F32)
    S.pbr = S.sb("pbr", [128, 12], F32)
    S.hzr = S.sb("hzr", [128, 2], F32)
    S.gcols = S.sb("gcols", [128, 32], F32)
    S.nwcols = S.sb("nwcols", [128, 32], F32)
    S.scol = S.sb("scol", [128, 32], F32)
    S.bcol = S.sb("bcol", [128, 32], F32)
    S.cbq1 = S.sb("cbq1", [128, 8], F32)
    S.cbq2 = S.sb("cbq2", [128, 8], F32)
    S.cbk1 = S.sb("cbk1", [128, 2], F32)
    S.cbk2 = S.sb("cbk2", [128, 2], F32)
    S.qm = S.sb("qm", [128, TOK], BF16)
    S.km = S.sb("km", [128, TOK], BF16)
    S.rowsA = S.sb("rowsA", [1, TOK], F32)
    S.rowsB = S.sb("rowsB", [1, TOK], F32)
    S.c1neg = S.sb("c1neg", [128, 32], F32)
    S.a1pt = S.sb("a1pt", [1, 16], I32)

    scr = ctx.enter_context(nc.sbuf_tensor("pscratch", [128, 12384], mybir.dt.uint8))
    scr_base = nc.lookup_mloc(scr).addr
    S.sb_at = lambda name, shape, off, dt=BF16: nc.alloc_sbuf_tensor_at(
        name, shape, dt, offset=scr_base + off)
    S.tr0 = S.sb_at("tr0", [1, TOK], 0, F32)
    S.qrow = S.sb_at("qrow", [1, TOK], 4096, F32)
    S.trow = S.sb_at("trow", [1, TOK], 8192, F32)
    S.mx48 = S.sb_at("mx48", [1, 64], 4480, F32)
    S.l1mtmp = S.sb_at("l1mtmp", [1, 1], 4736, F32)

    S.p512 = [S.psum(f"p512_{k}", [128, 512]) for k in range(4)]
    S.pbig = [S.psum(f"pbig_{k}", [128, 512]) for k in range(3)]
    S.pbf = S.psum("pbf", [128, 1024], BF16)
    S.prow = [S.pbig[0][0:1, :], S.pbig[1][0:1, :], S.pbig[2][0:1, :],
              S.p512[3][0:1, :]]

    A = S.arena
    S.av12 = A[:, 0:12384].rearrange("p (c t) -> p c t", c=12)
    S.q0 = A[:, 0:8256].rearrange("p (c t) -> p c t", c=8)
    S.k0 = A[:, 8256:10320].rearrange("p (c t) -> p c t", c=2)
    S.v0 = A[:, 10320:12384].rearrange("p (c t) -> p c t", c=2)
    S.xb2 = A[:, 0:2048].rearrange("p (c t) -> p c t", c=2)
    S.sq2 = A[:, 2048:4096].rearrange("p (c t) -> p c t", c=2)
    S.q1 = S.xn[:, 0:8224].rearrange("p (c t) -> p c t", c=8)
    S.k1 = S.xn[:, 8224:10280].rearrange("p (c t) -> p c t", c=2)
    S.q2 = A[:, 0:8192].rearrange("p (c t) -> p c t", c=8)
    S.k2 = A[:, 8256:10304].rearrange("p (c t) -> p c t", c=2)
    return S


def finalize(S, phases):
    nc = S.nc
    for ph in phases:
        with nc.Block() as block:
            for engname in ENGS:
                if ph.prog[engname] or (engname == "sync" and ph.drain):
                    getattr(block, engname)(
                        (lambda ph_, n_: lambda e: ph_._emit(n_, e))(ph, engname))
        nc.all_engine_barrier()
    S.ctx.close()
    return nc



def make_p0(S):
    sc = Script(S, "p0")
    st = sc.step
    irow = S.sb_at("p0_irow", [1, 128], 0, F32)
    ktb = S.sb_at("p0_ktb", [1, 4], 512, F32)

    for c in range(NRES):
        st("sync", lambda S, e, c=c: e.dma_start(
            out=S.xs[:, c * TOK:(c + 1) * TOK],
            in_=S.x[c * 128:(c + 1) * 128, :]), dma=True)
    for c in range(NRES, NCH):
        st("sync", lambda S, e, c=c: e.dma_start(
            out=S.xpark[c - NRES], in_=S.x[c * 128:(c + 1) * 128, :]), dma=True)
    for dst, src in ((S.gcols, S.gamma), (S.nwcols, S.nwc), (S.scol, S.scales),
                     (S.bcol, S.fbias), (S.cbq1, S.cb_q1), (S.cbq2, S.cb_q2),
                     (S.cbk1, S.cb_k1), (S.cbk2, S.cb_k2)):
        st("sync", lambda S, e, dst=dst, src=src: e.dma_start(
            out=dst[:], in_=src[:]), dma=True)

    g_ir = st("gpsimd", lambda S, e: e.iota(
        irow.rearrange("o (a j ee) -> o a j ee", a=2, j=32)[:],
        pattern=[[0, 2], [1, 32], [0, 2]], base=0, channel_multiplier=0,
        allow_small_or_imprecise_dtypes=True))
    g_id = st("gpsimd", lambda S, e: e.iota(
        S.xn[:, 0:128], pattern=[[1, 128]], base=0, channel_multiplier=-1,
        allow_small_or_imprecise_dtypes=True))
    g_pc = st("gpsimd", lambda S, e: e.iota(
        S.posc[:, 0:1], pattern=[[0, 1]], base=0, channel_multiplier=1,
        allow_small_or_imprecise_dtypes=True))
    g_sg = st("gpsimd", lambda S, e: e.iota(
        S.signr.rearrange("p (j ee) -> p j ee", j=64)[:],
        pattern=[[0, 64], [1, 2]], base=0, channel_multiplier=0,
        allow_small_or_imprecise_dtypes=True))
    g_pb = st("gpsimd", lambda S, e: e.dma_start(
        out=S.pbr[:], in_=S.pbase[0:1, 0:12].broadcast_to([128, 12])), dma=True)
    g_hz = st("gpsimd", lambda S, e: e.dma_start(
        out=S.hzr[:], in_=S.hz[:]), dma=True)
    g_kt = st("gpsimd", lambda S, e: e.dma_start(
        out=ktb[:, 0:1], in_=S.key_temp[:]), dma=True)

    st("vector", lambda S, e: e.memset(S.ones_c[:], 1.0))
    st("vector", lambda S, e: e.memset(S.ones_cf[:], 1.0))
    st("vector", lambda S, e: e.memset(S.ones_r32[:], 1.0))
    st("vector", lambda S, e: e.tensor_scalar(
        out=S.identb[:], in0=S.xn[:, 0:128], scalar1=0.0, scalar2=None,
        op0=ALU.is_equal), deps=[g_id])
    st("vector", lambda S, e: e.tensor_scalar(
        out=S.identf[:], in0=S.xn[:, 0:128], scalar1=0.0, scalar2=None,
        op0=ALU.is_equal))
    st("vector", lambda S, e: e.tensor_scalar(
        out=S.signr[:], in0=S.signr[:], scalar1=-2.0, scalar2=1.0,
        op0=ALU.mult, op1=ALU.add), deps=[g_sg])
    st("vector", lambda S, e: e.tensor_scalar(
        out=ktb[:, 3:4], in0=ktb[:, 0:1], scalar1=0.125, scalar2=None,
        op0=ALU.mult), deps=[g_kt])
    st("vector", lambda S, e: e.scalar_tensor_tensor(
        out=ktb[:, 0:1], in0=ktb[:, 0:1], scalar=-1.0, in1=ktb[:, 0:1],
        op0=ALU.mult, op1=ALU.max))
    st("vector", lambda S, e: e.tensor_scalar(
        out=ktb[:, 0:1], in0=ktb[:, 0:1], scalar1=0.125 * 1.02, scalar2=None,
        op0=ALU.mult))
    st("vector", lambda S, e: e.tensor_scalar(
        out=ktb[:, 1:2], in0=ktb[:, 0:1], scalar1=-1.0, scalar2=None,
        op0=ALU.mult))
    v_kt = st("vector", lambda S, e: e.memset(ktb[:, 2:3], 0.0))
    for t in range(7, -1, -1):
        st("vector", lambda S, e, t=t: e.tensor_tensor(
            out=S.posc[:, t:t + 1], in0=S.posc[:, 0:1], in1=S.pbr[:, t:t + 1],
            op=ALU.add), deps=[g_pc, g_pb])

    s_exp = st("scalar", lambda S, e: e.activation(
        out=irow[:], in_=irow[:], func=AF.Exp,
        scale=-math.log(10000.0) / 32.0), deps=[g_ir])

    rA = st("sync", lambda S, e: e.dma_start(
        out=S.rt[0:1, 0:128], in_=irow[:]), deps=[s_exp], dma=True)
    rB = st("sync", lambda S, e: e.dma_start(
        out=S.rt[1:2, 0:4], in_=ktb[:]), deps=[v_kt], dma=True)
    st("gpsimd", lambda S, e: e.dma_start(
        out=S.invfr[:], in_=S.rt[0:1, 0:128].broadcast_to([128, 128])),
        deps=[rA], dma=True)
    st("gpsimd", lambda S, e: e.dma_start(
        out=S.ktep[:], in_=S.rt[1:2, 3:4].broadcast_to([128, 1])),
        deps=[rB], dma=True)
    st("gpsimd", lambda S, e: e.dma_start(
        out=S.nktc[:], in_=S.rt[1:2, 1:2].broadcast_to([128, 1])),
        deps=[rB], dma=True)
    return sc



def make_lnstats(S, tag, mode="ln"):
    sc = Script(S, tag)
    st = sc.step
    ctok, sqtok, mmx, mmq, pk = {}, {}, {}, {}, {}
    for c in range(NCH):
        if c >= NRES:
            pk[c] = st("sync", lambda S, e, c=c: e.dma_start(
                out=S.xw[c % 2][:], in_=S.xpark[c - NRES]),
                deps=[ctok.get(c - 2) if c - 2 >= NRES else None,
                      sqtok.get(c - 2) if c - 2 >= NRES else None], dma=True)
        src = (S.xs[:, c * TOK:(c + 1) * TOK] if c < NRES else S.xw[c % 2][:])
        ctok[c] = st("vector", lambda S, e, c=c, src=src: e.tensor_copy(
            S.xb2[:, c % 2, :], src), deps=[pk.get(c), mmx.get(c - 2)])
        sqtok[c] = st("scalar", lambda S, e, c=c, src=src: e.activation(
            out=S.sq2[:, c % 2, :], in_=src, func=AF.Square),
            deps=[pk.get(c), mmq.get(c - 2)])
        st("tensor", lambda S, e, c=c: e.matmul(
            S.prow[0], S.ones_c[:], S.xb2[:, c % 2, 0:512],
            start=(c == 0), stop=(c == NCH - 1)), deps=[ctok[c]])
        mmx[c] = st("tensor", lambda S, e, c=c: e.matmul(
            S.prow[1], S.ones_c[:], S.xb2[:, c % 2, 512:1024],
            start=(c == 0), stop=(c == NCH - 1)))
        st("tensor", lambda S, e, c=c: e.matmul(
            S.prow[2], S.ones_c[:], S.sq2[:, c % 2, 0:512],
            start=(c == 0), stop=(c == NCH - 1)), deps=[sqtok[c]])
        mmq[c] = st("tensor", lambda S, e, c=c: e.matmul(
            S.prow[3], S.ones_c[:], S.sq2[:, c % 2, 512:1024],
            start=(c == 0), stop=(c == NCH - 1)))

    v = lambda fn, deps=(): st("vector", fn, deps=deps)
    v(lambda S, e: e.tensor_copy(S.tr0[:, 0:512], S.prow[0]), deps=[mmx[NCH - 1]])
    v(lambda S, e: e.tensor_copy(S.tr0[:, 512:1024], S.prow[1]))
    v(lambda S, e: e.tensor_copy(S.qrow[:, 0:512], S.prow[2]), deps=[mmq[NCH - 1]])
    v(lambda S, e: e.tensor_copy(S.qrow[:, 512:1024], S.prow[3]))
    v(lambda S, e: e.tensor_scalar(out=S.tr0[:], in0=S.tr0[:], scalar1=1.0 / D,
                                   scalar2=None, op0=ALU.mult))
    v(lambda S, e: e.tensor_scalar(out=S.qrow[:], in0=S.qrow[:], scalar1=1.0 / D,
                                   scalar2=None, op0=ALU.mult))
    v(lambda S, e: e.tensor_tensor(out=S.rowsB[:], in0=S.tr0[:], in1=S.tr0[:],
                                   op=ALU.mult))
    v(lambda S, e: e.tensor_tensor(out=S.qrow[:], in0=S.qrow[:], in1=S.rowsB[:],
                                   op=ALU.subtract))
    tvar = v(lambda S, e: e.tensor_scalar(out=S.qrow[:], in0=S.qrow[:],
                                          scalar1=EPS_LN, scalar2=None, op0=ALU.add))
    sA = st("scalar", lambda S, e: e.activation(out=S.rowsA[:], in_=S.qrow[:],
                                                func=AF.Sqrt), deps=[tvar])
    v(lambda S, e: e.reciprocal(out=S.rowsA[:], in_=S.rowsA[:]), deps=[sA])
    v(lambda S, e: e.tensor_tensor(out=S.rowsB[:], in0=S.tr0[:], in1=S.rowsA[:],
                                   op=ALU.mult))
    v(lambda S, e: e.tensor_scalar(out=S.rowsB[:], in0=S.rowsB[:], scalar1=-1.0,
                                   scalar2=None, op0=ALU.mult))
    if mode == "ln+rms":
        v(lambda S, e: e.tensor_scalar(out=S.qrow[:], in0=S.qrow[:],
                                       scalar1=-EPS_LN, scalar2=None, op0=ALU.add))
        v(lambda S, e: e.tensor_tensor(out=S.trow[:], in0=S.tr0[:], in1=S.tr0[:],
                                       op=ALU.mult))
        v(lambda S, e: e.tensor_tensor(out=S.qrow[:], in0=S.qrow[:], in1=S.trow[:],
                                       op=ALU.add))
        v(lambda S, e: e.tensor_tensor(out=S.trow[:], in0=S.rowsA[:], in1=S.rowsA[:],
                                       op=ALU.mult))
        v(lambda S, e: e.tensor_tensor(out=S.qrow[:], in0=S.qrow[:], in1=S.trow[:],
                                       op=ALU.mult))
        v(lambda S, e: e.tensor_tensor(out=S.trow[:], in0=S.rowsA[:], in1=S.rowsB[:],
                                       op=ALU.mult))
        v(lambda S, e: e.tensor_tensor(out=S.trow[:], in0=S.trow[:], in1=S.tr0[:],
                                       op=ALU.mult))
        v(lambda S, e: e.tensor_scalar(out=S.trow[:], in0=S.trow[:], scalar1=2.0,
                                       scalar2=None, op0=ALU.mult))
        v(lambda S, e: e.tensor_tensor(out=S.qrow[:], in0=S.qrow[:], in1=S.trow[:],
                                       op=ALU.add))
        v(lambda S, e: e.tensor_tensor(out=S.trow[:], in0=S.rowsB[:], in1=S.rowsB[:],
                                       op=ALU.mult))
        v(lambda S, e: e.tensor_tensor(out=S.qrow[:], in0=S.qrow[:], in1=S.trow[:],
                                       op=ALU.add))
        trms = v(lambda S, e: e.tensor_scalar(out=S.qrow[:], in0=S.qrow[:],
                                              scalar1=EPS_RMS, scalar2=None,
                                              op0=ALU.add))
        sT = st("scalar", lambda S, e: e.activation(out=S.trow[:], in_=S.qrow[:],
                                                    func=AF.Sqrt), deps=[trms])
        v(lambda S, e: e.reciprocal(out=S.trow[:], in_=S.trow[:]), deps=[sT])
        v(lambda S, e: e.tensor_tensor(out=S.rowsA[:], in0=S.rowsA[:],
                                       in1=S.trow[:], op=ALU.mult))
        v(lambda S, e: e.tensor_tensor(out=S.rowsB[:], in0=S.rowsB[:],
                                       in1=S.trow[:], op=ALU.mult))
    return sc



def _apw(ap):
    n = 1
    for d_ in ap.shape[1:]:
        n *= d_
    return n


def evac_residual(scale=None, bias=None):
    state = {"n": 0, "store": {}}

    def evac(S, sc, half, j, oc, deps):
        st = sc.step
        ps = S.p512[j % 2]
        lo = OWNL[half]
        pkdep = []
        if oc < NRES:
            dst = S.xs[:, oc * TOK + lo: oc * TOK + lo + 512]
        else:
            n = state["n"]
            state["n"] += 1
            sl = n % 2
            pk = st("sync", lambda S, e, sl=sl, oc=oc, lo=lo: e.dma_start(
                out=S.xr[sl][:], in_=S.xpark[oc - NRES, :, lo:lo + 512]),
                deps=[state["store"].get(n - 2)], dma=True)
            dst = S.xr[sl][:]
            pkdep = [pk]
        if scale is None:
            v1 = st("vector", lambda S, e, ps=ps, dst=dst: e.tensor_tensor(
                out=dst, in0=ps[:], in1=dst, op=ALU.add), deps=list(deps) + pkdep)
        else:
            v1 = st("vector", lambda S, e, ps=ps, dst=dst, oc=oc: e.scalar_tensor_tensor(
                out=dst, in0=ps[:], scalar=scale(S, oc), in1=dst,
                op0=ALU.mult, op1=ALU.add), deps=list(deps) + pkdep)
        last = v1
        if bias is not None:
            last = st("vector", lambda S, e, dst=dst, oc=oc: e.tensor_single_scalar(
                dst, dst, bias(S, oc), ALU.add))
        if oc >= NRES:
            state["store"][n] = st("sync", lambda S, e, sl=sl, oc=oc, lo=lo: e.dma_start(
                out=S.xpark[oc - NRES, :, lo:lo + 512], in_=S.xr[sl][:]),
                deps=[last], dma=True)
        return v1
    return evac


def evac_qkv():
    def evac(S, sc, half, j, oc, deps):
        st = sc.step
        ps = S.p512[j % 2]
        base = 516 * half + 4
        c1 = st("vector", lambda S, e, ps=ps, oc=oc, base=base: e.tensor_copy(
            S.av12[:, oc, base:base + 512], ps[:]), deps=deps)
        if oc >= 10:
            return c1
        acc = S.xr[0] if oc < 8 else S.xr[1]
        if oc in (0, 8):
            a1 = st("vector", lambda S, e, ps=ps, acc=acc: e.tensor_copy(
                acc[:], ps[:]))
        else:
            a1 = st("vector", lambda S, e, ps=ps, acc=acc: e.tensor_tensor(
                out=acc[:], in0=acc[:], in1=ps[:], op=ALU.add))
        if oc == 7:
            st("vector", lambda S, e, half=half: e.tensor_copy(
                S.qm[:, half * 512:half * 512 + 512], S.xr[0][:]))
        elif oc == 9:
            st("vector", lambda S, e, half=half: e.tensor_copy(
                S.km[:, half * 512:half * 512 + 512], S.xr[1][:]))
        return a1
    return evac


def make_proj(S, jobs, tag, post_nw=False, weng="gpsimd"):
    sc = Script(S, tag)
    st = sc.step
    noc = sum(jb[1] for jb in jobs)
    mm_first, mm_last, ev, atok = {}, {}, {}, {}
    j = 0
    for half in range(2):
        lo = OWNL[half]
        bA = st("tensor", lambda S, e, lo=lo: e.matmul(
            S.p512[2][:], S.ones_r32[:], S.rowsA[:, lo:lo + 512],
            start=True, stop=True),
            deps=[atok.get((0, NCH - 1))] if half == 1 else [])
        bB = st("tensor", lambda S, e, lo=lo: e.matmul(
            S.p512[3][:], S.ones_r32[:], S.rowsB[:, lo:lo + 512],
            start=True, stop=True))
        for c in range(NCH):
            deps = [bB]
            if c >= NRES:
                n = half * NPARK + (c - NRES)
                wdep = None
                if n >= 2:
                    h2, c2 = (n - 2) // NPARK, NRES + (n - 2) % NPARK
                    wdep = atok[(h2, c2)]
                ap = st("sync", lambda S, e, c=c: e.dma_start(
                    out=S.xw[c % 2][:], in_=S.xpark[c - NRES]),
                    deps=[wdep], dma=True)
                deps.append(ap)
            if half == 1:
                deps.append(mm_last.get(noc - 1))
            xsl = (S.xs[:, c * TOK + lo: c * TOK + lo + 512] if c < NRES
                   else S.xw[c % 2][:, lo:lo + 512])
            nsl = S.xn[:, c * 512:(c + 1) * 512]
            st("vector", lambda S, e, nsl=nsl, xsl=xsl: e.tensor_tensor(
                out=nsl, in0=xsl, in1=S.p512[2][:], op=ALU.mult), deps=deps)
            a2 = st("vector", lambda S, e, nsl=nsl: e.tensor_tensor(
                out=nsl, in0=nsl, in1=S.p512[3][:], op=ALU.add))
            if post_nw:
                a2 = st("vector", lambda S, e, nsl=nsl, c=c: e.tensor_single_scalar(
                    nsl, nsl, S.nwcols[:, c:c + 1], ALU.mult))
            atok[(half, c)] = a2
        for slabs, n_oc, evac in jobs:
            for oc in range(n_oc):
                s = j % 2
                ap = slabs[oc]
                kch = _apw(ap) // 128
                wb = st(weng, lambda S, e, s=s, ap=ap, w=_apw(ap): e.dma_start(
                    out=S.wb[s][:, 0:w], in_=ap),
                    deps=[mm_first.get(j - 1)], dma=True)
                for c in range(kch):
                    deps = [wb, atok[(half, c)]]
                    if c == 0:
                        deps.append(ev.get(j - 2))
                    mm = st("tensor", lambda S, e, s=s, c=c, j=j, kch=kch: e.matmul(
                        S.p512[j % 2][:], S.wb[s][:, c * 128:(c + 1) * 128],
                        S.xn[:, c * 512:(c + 1) * 512],
                        start=(c == 0), stop=(c == kch - 1)), deps=deps)
                    if c == 0:
                        mm_first[j] = mm
                mm_last[j] = mm
                ev[j] = evac(S, sc, half, j, oc, deps=[mm_last[j]])
                j += 1
    return sc


def make_ao_proj(S, slab, n_oc, kch, src, tag, weng="gpsimd"):
    sc = Script(S, tag)
    st = sc.step
    evac = evac_residual()
    mm_first, mm_last, ev = {}, {}, {}
    j = 0
    for half in range(2):
        lo = OWNL[half]
        rtok = {}
        for c in range(kch):
            rtok[c] = st("sync", lambda S, e, c=c, lo=lo: e.dma_start(
                out=S.xn[:, c * 512:(c + 1) * 512], in_=src[c, :, lo:lo + 512]),
                deps=[mm_last.get(n_oc - 1)] if half == 1 else [], dma=True)
        for oc in range(n_oc):
            s = j % 2
            wb = st(weng, lambda S, e, s=s, oc=oc: e.dma_start(
                out=S.wb[s][:, 0:kch * 128], in_=slab[oc]),
                deps=[mm_first.get(j - 1)], dma=True)
            for c in range(kch):
                deps = [wb, rtok[c]]
                if c == 0:
                    deps.append(ev.get(j - 2))
                mm = st("tensor", lambda S, e, s=s, c=c, j=j: e.matmul(
                    S.p512[j % 2][:], S.wb[s][:, c * 128:(c + 1) * 128],
                    S.xn[:, c * 512:(c + 1) * 512],
                    start=(c == 0), stop=(c == kch - 1)), deps=deps)
                if c == 0:
                    mm_first[j] = mm
            mm_last[j] = mm
            ev[j] = evac(S, sc, half, j, oc, deps=[mm_last[j]])
            j += 1
    return sc


def make_l1proj(S, weng="gpsimd"):
    sc = Script(S, "l1p")
    st = sc.step
    jobs = [
        (S.l1qT, 32, lambda S_, half, oc, lo: S_.q1d[oc, :, lo:lo + 512]),
        (S.l1kT, 8, lambda S_, half, oc, lo: S_.k1c[half, oc]),
        (S.l1vT, 8, lambda S_, half, oc, lo: S_.v1f[oc, :, lo:lo + 512]),
    ]
    stg = [S.arena[:, k * 512:(k + 1) * 512] for k in range(2)]
    mm_first, mm_last, ev, atok, evo = {}, {}, {}, {}, {}
    j = 0
    for half in range(2):
        lo = OWNL[half]
        bA = st("tensor", lambda S, e, lo=lo: e.matmul(
            S.p512[2][:], S.ones_r32[:], S.rowsA[:, lo:lo + 512],
            start=True, stop=True),
            deps=[atok.get((0, NCH - 1))] if half == 1 else [])
        bB = st("tensor", lambda S, e, lo=lo: e.matmul(
            S.p512[3][:], S.ones_r32[:], S.rowsB[:, lo:lo + 512],
            start=True, stop=True))
        for c in range(NCH):
            deps = [bB]
            if c >= NRES:
                n = half * NPARK + (c - NRES)
                wdep = None
                if n >= 2:
                    h2, c2 = (n - 2) // NPARK, NRES + (n - 2) % NPARK
                    wdep = atok[(h2, c2)]
                ap = st("sync", lambda S, e, c=c: e.dma_start(
                    out=S.xw[c % 2][:], in_=S.xpark[c - NRES]),
                    deps=[wdep], dma=True)
                deps.append(ap)
            if half == 1:
                deps.append(mm_last.get(47))
            xsl = (S.xs[:, c * TOK + lo: c * TOK + lo + 512] if c < NRES
                   else S.xw[c % 2][:, lo:lo + 512])
            nsl = S.xn[:, c * 512:(c + 1) * 512]
            st("vector", lambda S, e, nsl=nsl, xsl=xsl: e.tensor_tensor(
                out=nsl, in0=xsl, in1=S.p512[2][:], op=ALU.mult), deps=deps)
            atok[(half, c)] = st("vector", lambda S, e, nsl=nsl: e.tensor_tensor(
                out=nsl, in0=nsl, in1=S.p512[3][:], op=ALU.add))
        for ji, (slab, n_oc, dstd) in enumerate(jobs):
            for oc in range(n_oc):
                s = j % 2
                wb = st(weng, lambda S, e, s=s, slab=slab, oc=oc: e.dma_start(
                    out=S.wb[s][:], in_=slab[oc]),
                    deps=[mm_first.get(j - 1)], dma=True)
                for c in range(NCH):
                    deps = [wb, atok[(half, c)]]
                    if c == 0:
                        deps.append(ev.get(j - 2))
                    mm = st("tensor", lambda S, e, s=s, c=c, j=j: e.matmul(
                        S.p512[j % 2][:], S.wb[s][:, c * 128:(c + 1) * 128],
                        S.xn[:, c * 512:(c + 1) * 512],
                        start=(c == 0), stop=(c == NCH - 1)), deps=deps)
                    if c == 0:
                        mm_first[j] = mm
                mm_last[j] = mm
                v1 = st("vector", lambda S, e, j=j: e.tensor_copy(
                    stg[j % 2], S.p512[j % 2][:]),
                    deps=[mm_last[j], evo.get(j - 2)])
                ev[j] = v1
                evo[j] = st("sync", lambda S, e, half=half, dstd=dstd, oc=oc, j=j, lo=lo:
                            e.dma_start(out=dstd(S, half, oc, lo), in_=stg[j % 2]),
                            deps=[v1], dma=True)
                j += 1
    return sc



def make_halocc(S):
    sc = Script(S, "halo")
    st = sc.step
    hqgf = S.hqg.rearrange("r s c p f -> (r s c) p f")
    sts = []
    for sl in range(2):
        for cc in range(12):
            sts.append(st("sync", lambda S, e, sl=sl, cc=cc: e.dma_start(
                out=S.hqc[sl, cc], in_=S.av12[:, cc, 516 * sl + 512:516 * sl + 516]),
                dma=True))
    ag = st("gpsimd", lambda S, e: e.collective_compute(
        "AllGather", ALU.bypass, replica_groups=[list(range(8))],
        ins=[S.hqc[:]], outs=[S.hqg[:]]), deps=sts, coll=True)
    pl = st("gpsimd", lambda S, e: e.dma_start(out=S.a1pt[:], in_=S.prm[:]), dma=True)
    for qb in range(2):
        stage = S.xn[:, qb * 64:qb * 64 + 48]

        def _ld(S, e, qb=qb, stage=stage):
            off = e.value_load(S.a1pt[0:1, qb:qb + 1])
            return e.dma_start(
                out=stage.rearrange("p (c f) -> p c f", c=12),
                in_=hqgf[bass.ds(off, 12)].rearrange("c p f -> p c f"))
        ld = st("sync", _ld, deps=[ag, pl], dma=True)
        mk = st("vector", lambda S, e, qb=qb, stage=stage: e.tensor_single_scalar(
            stage, stage, S.hzr[:, qb:qb + 1], ALU.mult), deps=[ld])
        for cc in range(12):
            st("vector", lambda S, e, qb=qb, cc=cc, stage=stage: e.tensor_copy(
                S.av12[:, cc, 516 * qb:516 * qb + 4], stage[:, 4 * cc:4 * cc + 4]),
                deps=[mk])
    return sc


def q1dst(S, oc, b, kind):
    if kind == "pre":
        return S.q1[:, oc, 514 * b:514 * b + 2]
    return S.q1[:, oc, 514 * b + 2:514 * b + 514]


def k1dst(S, oc, b, kind):
    if kind == "pre":
        return S.k1[:, oc, 514 * b:514 * b + 2]
    return S.k1[:, oc, 514 * b + 2:514 * b + 514]


def q2dst(S, oc, b, kind):
    return S.q2[:, oc, 512 * b:512 * b + 512]


def k2dst(S, oc, b, kind):
    return S.k2[:, oc, 512 * b:512 * b + 512]


def make_conv(S, jobs, pre2, tag):
    sc = Script(S, tag)
    st = sc.step
    pown = [S.p512[0], S.p512[1]]
    ppre = [S.p512[2][:, 0:2], S.p512[3][:, 0:2]] if pre2 else None
    mm_first, lastev = {}, {}
    j = 0
    for slabs, n_oc, kch, src, dst, bias, ibase in jobs:
        for oc in range(n_oc):
            s = j % 2
            ap = slabs(oc)
            wb = st("gpsimd", lambda S, e, s=s, ap=ap, w=_apw(ap): e.dma_start(
                out=S.wb[s][:, 0:w], in_=ap),
                deps=[mm_first.get(j - 1)], dma=True)
            gi = 0
            for b in range(2):
                segs = ([("pre", 2)] if pre2 else []) + [("own", 512)]
                for kind, wd in segs:
                    ps = (pown if kind == "own" else ppre)[j % 2]
                    nmm = 0
                    for tau in range(3):
                        for c in range(kch):
                            ilo = ibase(b) + tau + (2 if (kind == "own" and pre2) else 0)
                            deps = [wb]
                            if nmm == 0:
                                deps.append(lastev.get((kind, j % 2)))
                            mm = st("tensor", lambda S, e, s=s, ps=ps, wd=wd, tau=tau,
                                    c=c, kch=kch, ilo=ilo, nmm=nmm, oc=oc, src=src:
                                    e.matmul(ps[:, 0:wd],
                                             S.wb[s][:, (tau * kch + c) * 128:
                                                     (tau * kch + c + 1) * 128],
                                             src(S, c, oc)[:, ilo:ilo + wd],
                                             start=(nmm == 0),
                                             stop=(nmm == 3 * kch - 1)), deps=deps)
                            if j not in mm_first:
                                mm_first[j] = mm
                            nmm += 1
                    lastev[(kind, j % 2)] = st(
                        "vector", lambda S, e, ps=ps, wd=wd, oc=oc,
                        b=b, kind=kind, dst=dst, bias=bias:
                        e.tensor_single_scalar(
                            dst(S, oc, b, kind), ps[:, 0:wd],
                            bias(S, oc), ALU.add), deps=[mm])
                    gi += 1
            j += 1
    return sc


def make_meanfix(S):
    sc = Script(S, "mf")
    st = sc.step
    for bi, (buf, scale) in enumerate(((S.qm, 1.0 / 16.0), (S.km, 1.0 / 4.0))):
        stage = S.xn[0:64, bi * TOK:(bi + 1) * TOK]
        t1 = st("sync", lambda S, e, buf=buf, stage=stage: e.dma_start(
            out=stage, in_=buf[64:128, :]), dma=True)
        st("vector", lambda S, e, buf=buf, stage=stage: e.tensor_tensor(
            out=buf[0:64, :], in0=buf[0:64, :], in1=stage, op=ALU.add), deps=[t1])
        t3 = st("vector", lambda S, e, buf=buf, scale=scale: e.tensor_scalar(
            out=buf[0:64, :], in0=buf[0:64, :], scalar1=scale, scalar2=None,
            op0=ALU.mult))
        st("sync", lambda S, e, buf=buf: e.dma_start(
            out=buf[64:128, :], in_=buf[0:64, :]), deps=[t3], dma=True)
    return sc



def make_postproc(S):
    sc = Script(S, "pp")
    st = sc.step
    id32 = S.identf
    hs32 = S.sb_at("pp_hs32", [128, 2], 0, F32)
    hsT32 = S.sb_at("pp_hsT32", [2, 128], 32, F32)
    rn = S.sb_at("pp_rn", [2, 128], 544, F32)
    i32t = S.sb_at("pp_i32", [128, 128], 1056, I32)
    angf, angc = S.xr[0], S.xr[1]
    qxf, CCf = S.xw[0], S.xw[1]
    stg = S.xn[:, 0:1280].rearrange("p (c t) -> p c t", c=10)
    vtm = S.xn[:, 1280:1540]
    vtmv = S.xn[:, 1280:1540].rearrange("p (h u) -> p h u", u=65)
    pn2 = S.pbig[0][0:2, 0:128]
    pbc = S.pbig[1][:, 0:128]
    psT = S.pbig[2][:, 0:128]
    psV = S.pbf[:, 0:128]

    st("gpsimd", lambda S, e: e.memset(hs32[:], 0.0))
    st("gpsimd", lambda S, e: e.memset(hs32[0:64, 0:1], 1.0))
    g3 = st("gpsimd", lambda S, e: e.memset(hs32[64:128, 1:2], 1.0))
    T0 = st("tensor", lambda S, e: e.transpose(pn2, hs32[:], id32[:]), deps=[g3])
    e0 = st("vector", lambda S, e: e.tensor_copy(hsT32[:], pn2), deps=[T0])

    swm = []
    for i_ in range(16):
        swm += [2 * i_ + 1, 2 * i_]

    prev_h3, prev_h6 = e0, None
    prev_T1 = prev_T2 = prev_e2 = None
    prevV = None
    prev_vst = None
    prev_store = {}
    for t in range(8):
        qb, tl = t // 4, t % 4
        col = qb * 512 + tl * 128
        st("vector", lambda S, e, t=t: e.tensor_single_scalar(
            angf[:, 0:128], S.invfr[:], S.posc[:, t:t + 1], ALU.mult),
            deps=[prev_T2])
        st("vector", lambda S, e: e.tensor_scalar(
            out=i32t[:], in0=angf[:, 0:128], scalar1=1.0 / TWO_PI, scalar2=None,
            op0=ALU.mult))
        st("vector", lambda S, e: e.scalar_tensor_tensor(
            out=angf[:, 128:256], in0=i32t[:], scalar=-TWO_PI, in1=angf[:, 0:128],
            op0=ALU.mult, op1=ALU.add))
        st("vector", lambda S, e: e.tensor_scalar(
            out=angf[:, 128:256], in0=angf[:, 128:256], scalar1=0.0,
            scalar2=TWO_PI - 1e-5, op0=ALU.max, op1=ALU.min))
        a5 = st("vector", lambda S, e: e.tensor_scalar(
            out=angf[:, 128:256], in0=angf[:, 128:256], scalar1=PI, scalar2=None,
            op0=ALU.subtract))
        s1 = st("scalar", lambda S, e: e.activation(
            out=angf[:, 256:384], in_=angf[:, 128:256], func=AF.Sin), deps=[a5])
        st("vector", lambda S, e: e.tensor_scalar(
            out=angc[:, 0:128], in0=angf[:, 0:128], scalar1=PI / 2.0, scalar2=None,
            op0=ALU.add), deps=[prev_T1])
        st("vector", lambda S, e: e.tensor_scalar(
            out=i32t[:], in0=angc[:, 0:128], scalar1=1.0 / TWO_PI, scalar2=None,
            op0=ALU.mult))
        st("vector", lambda S, e: e.scalar_tensor_tensor(
            out=angc[:, 128:256], in0=i32t[:], scalar=-TWO_PI, in1=angc[:, 0:128],
            op0=ALU.mult, op1=ALU.add))
        st("vector", lambda S, e: e.tensor_scalar(
            out=angc[:, 128:256], in0=angc[:, 128:256], scalar1=0.0,
            scalar2=TWO_PI - 1e-5, op0=ALU.max, op1=ALU.min))
        c5 = st("vector", lambda S, e: e.tensor_scalar(
            out=angc[:, 128:256], in0=angc[:, 128:256], scalar1=PI, scalar2=None,
            op0=ALU.subtract))
        s2 = st("scalar", lambda S, e: e.activation(
            out=angc[:, 256:384], in_=angc[:, 128:256], func=AF.Sin), deps=[c5])
        b1 = st("vector", lambda S, e: e.tensor_tensor(
            out=angf[:, 384:512], in0=angf[:, 256:384], in1=S.signr[:],
            op=ALU.mult), deps=[s1, prev_T2])
        b2 = st("vector", lambda S, e: e.tensor_scalar(
            out=angc[:, 384:512], in0=angc[:, 256:384], scalar1=-1.0, scalar2=None,
            op0=ALU.mult), deps=[s2, prev_T1])
        T1 = st("tensor", lambda S, e: e.transpose(psT, angc[:, 384:512], id32[:]),
                deps=[b2, prev_e2])
        e1 = st("vector", lambda S, e: e.tensor_copy(CCf[:, 0:128], psT), deps=[T1])
        T2 = st("tensor", lambda S, e: e.transpose(psT, angf[:, 384:512], id32[:]),
                deps=[b1, e1])
        e2 = st("vector", lambda S, e: e.tensor_copy(CCf[:, 128:256], psT), deps=[T2])
        prev_T1, prev_T2, prev_e2 = T1, T2, e2
        for cc in range(10):
            is_q = cc < 8
            src = (S.q2 if is_q else S.k2)[:, cc if is_q else cc - 8, col:col + 128]
            mean = (S.km if is_q else S.qm)[:, col:col + 128]
            st("vector", lambda S, e, src=src, mean=mean: e.scalar_tensor_tensor(
                out=qxf[:, 0:128], in0=mean, scalar=0.5, in1=src,
                op0=ALU.mult, op1=ALU.add))
            h2 = st("vector", lambda S, e: e.tensor_tensor(
                out=qxf[:, 128:256], in0=qxf[:, 0:128], in1=qxf[:, 0:128],
                op=ALU.mult))
            M1 = st("tensor", lambda S, e: e.matmul(
                pn2, hs32[:], qxf[:, 128:256], start=True, stop=True),
                deps=[h2, prev_h3])
            h3 = st("vector", lambda S, e: e.tensor_scalar(
                out=rn[:], in0=pn2, scalar1=1e-24, scalar2=None, op0=ALU.max),
                deps=[M1])
            h4 = st("vector", lambda S, e: e.reciprocal(out=rn[:], in_=rn[:]))
            s3 = st("scalar", lambda S, e: e.activation(out=rn[:], in_=rn[:],
                                                        func=AF.Sqrt), deps=[h4])
            if not is_q:
                s3 = st("vector", lambda S, e: e.tensor_single_scalar(
                    rn[:], rn[:], S.ktep[0:2, 0:1], ALU.mult), deps=[s3])
            M2 = st("tensor", lambda S, e: e.matmul(
                pbc, hsT32[:], rn[:], start=True, stop=True),
                deps=[s3] + ([prev_h6] if prev_h6 else []))
            h6 = st("vector", lambda S, e: e.tensor_tensor(
                out=qxf[:, 0:128], in0=qxf[:, 0:128], in1=pbc, op=ALU.mult),
                deps=[M2])
            st("vector", lambda S, e: e.stream_shuffle(
                out=qxf[:, 128:256], in_=qxf[:, 0:128], mask=swm))
            st("vector", lambda S, e: e.tensor_tensor(
                out=qxf[:, 0:128], in0=qxf[:, 0:128], in1=CCf[:, 0:128],
                op=ALU.mult), deps=[e1])
            st("vector", lambda S, e: e.tensor_tensor(
                out=qxf[:, 128:256], in0=qxf[:, 128:256], in1=CCf[:, 128:256],
                op=ALU.mult), deps=[e2])
            h10 = st("vector", lambda S, e, cc=cc: e.tensor_tensor(
                out=stg[:, cc, :], in0=qxf[:, 0:128], in1=qxf[:, 128:256],
                op=ALU.add), deps=prev_store.get(cc, []))
            prev_h3, prev_h6 = h3, h6
            if is_q:
                prev_store[cc] = [st("sync", lambda S, e, cc=cc, col=col: e.dma_start(
                    out=S.qfd[cc, :, col:col + 128], in_=stg[:, cc, :]),
                    deps=[h10], dma=True)]
            else:
                kc = cc - 8
                s_a = st("sync", lambda S, e, kc=kc, col=col, cc=cc: e.dma_start(
                    out=S.k0c[2 * kc, :, col:col + 128],
                    in_=stg[0:64, cc, :]), deps=[h10], dma=True)
                s_b = st("sync", lambda S, e, kc=kc, col=col, cc=cc: e.dma_start(
                    out=S.k0c[2 * kc + 1, :, col:col + 128],
                    in_=stg[64:128, cc, :]), deps=[h10], dma=True)
                prev_store[cc] = [s_a, s_b]
        vton = st("vector", lambda S, e: e.memset(vtmv[:, 0:4, 64:65], 1.0),
                  deps=[prev_vst])
        cend = None
        for vc in range(2):
            scol = 4 + 516 * qb + tl * 128 - (1 if vc == 1 else 0)
            T3 = st("tensor", lambda S, e, vc=vc, scol=scol: e.transpose(
                psV, S.v0[:, vc, scol:scol + 128], S.identb[:]), deps=[prevV])
            c1 = st("vector", lambda S, e, vc=vc: e.tensor_copy(
                vtm[:, 65 * (2 * vc):65 * (2 * vc) + 64], psV[:, 0:64]),
                deps=[T3, prev_vst])
            cend = st("vector", lambda S, e, vc=vc: e.tensor_copy(
                vtm[:, 65 * (2 * vc + 1):65 * (2 * vc + 1) + 64], psV[:, 64:128]))
            prevV = cend
        prev_vst = st("sync", lambda S, e, col=col: e.dma_start(
            out=S.v0c[col:col + 128, :], in_=vtm[:, 0:260]),
            deps=[cend, vton], dma=True)
    return sc


def make_cc1(S):
    sc = Script(S, "cc1")
    sc.step("gpsimd", lambda S, e: e.collective_compute(
        "AllGather", ALU.bypass, replica_groups=[list(range(8))],
        ins=[S.k0c[:]], outs=[S.k0g[:]]), coll=True)
    sc.step("gpsimd", lambda S, e: e.collective_compute(
        "AllGather", ALU.bypass, replica_groups=[list(range(8))],
        ins=[S.v0c[:]], outs=[S.v0g[:]]), coll=True)
    return sc



def make_attn0(S):
    sc = Script(S, "at0")
    st = sc.step
    qt = [S.xn[0:64, i * 512:(i + 1) * 512] for i in range(4)]
    kb = [S.xn[0:64, 2048 + i * 128:2048 + (i + 1) * 128] for i in range(2)]
    vb = [S.xn[:, 2304 + i * 65:2304 + (i + 1) * 65] for i in range(2)]
    Pb = [S.xn[:, 2560 + i * 512:2560 + (i + 1) * 512] for i in range(4)]
    ost = [S.xn[0:64, 4608 + i * 512:4608 + (i + 1) * 512] for i in range(2)]
    iotar = S.sb_at("a0_iotar", [1, 512], 0, F32)
    kpcs = S.sb_at("a0_kpcs", [128, 64], 2048, F32)
    qposr = S.sb_at("a0_qposr", [1, 512], 2304, F32)
    pS = [S.p512[0], S.p512[1]]
    qpbc = S.p512[2]
    pO = [S.pbig[0][0:65, :], S.pbig[1][0:65, :], S.pbig[2][0:65, :],
          S.p512[3][0:65, :]]
    bcb = [S.xr[1][:, 0:512], S.xw[0][:, 0:512]]

    g_io = st("gpsimd", lambda S, e: e.iota(
        iotar[:], pattern=[[1, 512]], base=0, channel_multiplier=0,
        allow_small_or_imprecise_dtypes=True))
    g_kp = st("gpsimd", lambda S, e: e.iota(
        kpcs[:], pattern=[[128, 64]], base=0, channel_multiplier=1,
        allow_small_or_imprecise_dtypes=True))
    lastq = [None] * 4
    lastk = [None, None]
    lastv = [None, None]
    lastex = [None, None]
    lastPb = [None] * 4
    lastnorm = [None] * 4
    lastbc = [None, None]
    last_mk = None
    last_qbc = None
    rt_r3 = [None] * 4
    last_r2 = None
    last_store = [None, None]
    rts = 0
    for qb in range(2):
        qp = st("vector", lambda S, e, qb=qb: e.tensor_single_scalar(
            qposr[:], iotar[:], S.pbr[0:1, 4 * qb:4 * qb + 1], ALU.add),
            deps=[g_io, last_qbc])
        qbc = st("tensor", lambda S, e: e.matmul(
            qpbc[:], S.ones_r32[:], qposr[:], start=True, stop=True),
            deps=[qp, last_mk])
        last_qbc = qbc
        for g in range(NKV0):
            qld = [None] * 4
            for i in range(4):
                h = 4 * g + i
                qld[i] = st("sync", lambda S, e, h=h, i=i, qb=qb: e.dma_start(
                    out=qt[i][:], in_=S.qfd[h // 2, (h % 2) * 64:(h % 2) * 64 + 64,
                                            qb * 512:qb * 512 + 512]),
                    deps=[lastq[i]], dma=True)
            pvq = [None] * 4
            for kt in range(KT0[qb]):
                r, slot = blk_owner(kt // 4)
                sl = kt % 2
                kld = st("sync", lambda S, e, r=r, g=g, slot=slot, kt=kt, sl=sl:
                         e.dma_start(out=kb[sl][:],
                                     in_=S.k0g[r, g, :,
                                               slot * 512 + (kt % 4) * 128:
                                               slot * 512 + (kt % 4) * 128 + 128]),
                         deps=[lastk[sl]], dma=True)
                vld = st("sync", lambda S, e, r=r, g=g, slot=slot, kt=kt, sl=sl:
                         e.dma_start(out=vb[sl][:],
                                     in_=S.v0g[r, slot * 512 + (kt % 4) * 128:
                                               slot * 512 + (kt % 4) * 128 + 128,
                                               65 * g:65 * g + 65]),
                         deps=[lastv[sl]], dma=True)
                for i in range(4):
                    psl = (4 * kt + i) % 2
                    ss = st("tensor", lambda S, e, sl=sl, i=i, psl=psl: e.matmul(
                        pS[psl][:], kb[sl][:], qt[i][:], start=True, stop=True),
                        deps=[kld, qld[i], lastex[psl]])
                    lastk[sl] = ss
                    lastq[i] = ss
                    ex = st("scalar", lambda S, e, i=i, psl=psl: e.activation(
                        out=Pb[i][:], in_=pS[psl][:], func=AF.Exp,
                        bias=S.nktc[:, 0:1]), deps=[ss, lastPb[i]])
                    lastex[psl] = ex
                    mk = st("vector", lambda S, e, i=i, kt=kt:
                            e.scalar_tensor_tensor(
                                out=Pb[i][:], in0=qpbc[:],
                                scalar=kpcs[:, kt:kt + 1], in1=Pb[i][:],
                                op0=ALU.is_ge, op1=ALU.mult),
                            deps=[ex, qbc, g_kp])
                    last_mk = mk
                    pde = [vld, mk]
                    if kt == 0:
                        pde.append(lastnorm[i])
                    pv = st("tensor", lambda S, e, sl=sl, i=i, kt=kt, qb=qb:
                            e.matmul(pO[i], vb[sl][:], Pb[i][:],
                                     start=(kt == 0), stop=(kt == KT0[qb] - 1)),
                            deps=pde)
                    lastv[sl] = pv
                    lastPb[i] = pv
                    pvq[i] = pv
            for i in range(4):
                h = 4 * g + i
                nsl = i % 2
                r1 = st("vector", lambda S, e, i=i: e.reciprocal(
                    out=S.xr[0][64:65, :], in_=pO[i][64:65, :]),
                    deps=[pvq[i], last_r2])
                slo = rts % 4
                rts += 1
                r2 = st("sync", lambda S, e, slo=slo: e.dma_start(
                    out=S.rt[slo:slo + 1, :], in_=S.xr[0][64:65, :]),
                    deps=[r1, rt_r3[slo]], dma=True)
                last_r2 = r2
                r3 = st("sync", lambda S, e, slo=slo, nsl=nsl: e.dma_start(
                    out=bcb[nsl][:],
                    in_=S.rt[slo:slo + 1, :].broadcast_to([128, 512])),
                    deps=[r2, lastbc[nsl]], dma=True)
                rt_r3[slo] = r3
                r4 = st("vector", lambda S, e, i=i, nsl=nsl: e.tensor_tensor(
                    out=ost[nsl][:], in0=pO[i][0:64, :], in1=bcb[nsl][0:64, :],
                    op=ALU.mult), deps=[r3, last_store[nsl]])
                so = st("sync", lambda S, e, h=h, qb=qb, nsl=nsl: e.dma_start(
                    out=S.ao0[h // 2, (h % 2) * 64:(h % 2) * 64 + 64,
                              qb * 512:qb * 512 + 512], in_=ost[nsl][:]),
                    deps=[r4], dma=True)
                last_store[nsl] = so
                lastnorm[i] = r4
                lastbc[nsl] = r4
    return sc


def make_v1build(S):
    sc = Script(S, "v1b")
    st = sc.step
    vbuf = [S.xn[:, i * 128:(i + 1) * 128] for i in range(2)]
    vstg = [S.xn[:, 256 + i * 129:256 + (i + 1) * 129] for i in range(2)]
    psV = S.pbf[:, 0:128]
    prev_ev = {}
    prev_st = {}
    for oc in range(8):
        for t in range(8):
            j = oc * 8 + t
            sl = j % 2
            ld = st("sync", lambda S, e, oc=oc, t=t, sl=sl: e.dma_start(
                out=vbuf[sl][:], in_=S.v1f[oc, :, t * 128:(t + 1) * 128]),
                deps=[prev_ev.get(sl)], dma=True)
            T = st("tensor", lambda S, e, sl=sl: e.transpose(
                psV, vbuf[sl][:], S.identb[:]), deps=[ld, prev_ev.get("ps")])
            c1 = st("vector", lambda S, e, sl=sl: e.tensor_copy(
                vstg[sl][:, 0:64], psV[:, 0:64]), deps=[T, prev_st.get(sl)])
            st("vector", lambda S, e, sl=sl: e.memset(vstg[sl][:, 64:65], 1.0))
            c3 = st("vector", lambda S, e, sl=sl: e.tensor_copy(
                vstg[sl][:, 65:129], psV[:, 64:128]))
            prev_ev[sl] = c3
            prev_ev["ps"] = c3
            prev_st[sl] = st("sync", lambda S, e, oc=oc, t=t, sl=sl: e.dma_start(
                out=S.v1c[t // 4, (t % 4) * 128:(t % 4) * 128 + 128,
                          129 * oc:129 * oc + 129],
                in_=vstg[sl][:]), deps=[c3], dma=True)
    return sc


def make_cc2(S):
    sc = Script(S, "cc2")
    st = sc.step
    items = [("q", h, hf) for h in range(32) for hf in range(2)] +             [("k", g, hf) for g in range(8) for hf in range(2)]
    xsl = [S.xn[:, i * 512:(i + 1) * 512] for i in range(2)]
    psr = [S.pbig[0][0:1, :], S.pbig[1][0:1, :]]
    mz = st("vector", lambda S, e: e.memset(S.mx48[:], 1e-30))
    lastred = [None, None]
    lastsq = [None, None]
    lastmm = [None, None]
    for i, (kind, idx, hf) in enumerate(items):
        sl = i % 2
        if kind == "q":
            src = S.q1d[idx, :, hf * 512:hf * 512 + 512]
            col = idx
        else:
            src = S.k1c[hf, idx]
            col = 32 + idx
        ld = st("sync", lambda S, e, src=src, sl=sl: e.dma_start(
            out=xsl[sl], in_=src), deps=[lastsq[sl], lastmm[sl]], dma=True)
        sq = st("vector", lambda S, e, sl=sl: e.tensor_tensor(
            out=xsl[sl], in0=xsl[sl], in1=xsl[sl], op=ALU.mult), deps=[ld])
        lastsq[sl] = sq
        mm = st("tensor", lambda S, e, sl=sl: e.matmul(
            psr[sl], S.ones_c[:], xsl[sl], start=True, stop=True),
            deps=[sq, ld, lastred[sl]])
        lastmm[sl] = mm
        rd = st("vector", lambda S, e, sl=sl: e.reduce_max(
            S.l1mtmp[:], psr[sl], axis=AX.X), deps=[mm])
        lastred[sl] = st("vector", lambda S, e, col=col: e.tensor_tensor(
            out=S.mx48[:, col:col + 1], in0=S.mx48[:, col:col + 1],
            in1=S.l1mtmp[:], op=ALU.max), deps=[mz])
    c2 = lastred[0]
    s1 = st("scalar", lambda S, e: e.activation(
        out=S.mx48[0:1, 0:48], in_=S.mx48[0:1, 0:48], func=AF.Sqrt),
        deps=[lastred[0], lastred[1]])
    d1 = st("sync", lambda S, e: e.dma_start(
        out=S.mx1c[0:1, 0:8], in_=S.mx48[0:1, 32:40]), deps=[s1], dma=True)
    cg = st("gpsimd", lambda S, e: e.collective_compute(
        "AllGather", ALU.bypass, replica_groups=[list(range(8))],
        ins=[S.mx1c[:]], outs=[S.mx1g[:]]), deps=[d1], coll=True)
    kg = st("gpsimd", lambda S, e: e.collective_compute(
        "AllGather", ALU.bypass, replica_groups=[list(range(8))],
        ins=[S.k1c[:]], outs=[S.k1g[:]]), coll=True)
    vg = st("gpsimd", lambda S, e: e.collective_compute(
        "AllGather", ALU.bypass, replica_groups=[list(range(8))],
        ins=[S.v1c[:]], outs=[S.v1g[:]]), coll=True)
    for qb in range(2):
        def _kcp(S, e, qb=qb):
            off = e.value_load(S.a1pt[0:1, 2 + qb:3 + qb])
            return e.dma_start(
                out=S.k1p.rearrange("s g p u -> s (g p u)")[qb:qb + 1, :],
                in_=S.k1g.rearrange("r g p u -> r (g p u)")[bass.ds(off, 1), :])
        st("sync", _kcp, deps=[kg], dma=True)

        def _vcp(S, e, qb=qb):
            off = e.value_load(S.a1pt[0:1, 2 + qb:3 + qb])
            return e.dma_start(
                out=S.v1p.rearrange("s u f -> s (u f)")[qb:qb + 1, :],
                in_=S.v1g.rearrange("r u f -> r (u f)")[bass.ds(off, 1), :])
        st("sync", _vcp, deps=[vg], dma=True)
    l1 = st("sync", lambda S, e: e.dma_start(
        out=S.xr[0][0:1, 0:64],
        in_=S.mx1g.rearrange("r c -> (r c)").rearrange("(o f) -> o f", o=1)),
        deps=[cg], dma=True)
    gt = l1
    for r in range(1, 8):
        gt = st("vector", lambda S, e, r=r: e.tensor_tensor(
            out=S.xr[0][0:1, 0:8], in0=S.xr[0][0:1, 0:8],
            in1=S.xr[0][0:1, 8 * r:8 * r + 8], op=ALU.max),
            deps=[gt] if r == 1 else [])
    s2 = st("sync", lambda S, e: e.dma_start(
        out=S.rt[6:7, 0:8], in_=S.xr[0][0:1, 0:8]), deps=[gt], dma=True)
    kxs = []
    for i in range(4):
        kxs.append(st("sync", lambda S, e, i=i: e.dma_start(
            out=S.xr[1][0:1, 64:96].rearrange(
                "o (g i2) -> o g i2", g=8)[:, :, i:i + 1]
            .rearrange("o g i2 -> o (g i2)"),
            in_=S.rt[6:7, 0:8], allow_slow_non_contiguous=True),
            deps=[s2], dma=True))
    cr = st("vector", lambda S, e: e.tensor_tensor(
        out=S.xr[0][0:1, 64:96], in0=S.mx48[0:1, 0:32],
        in1=S.xr[1][0:1, 64:96], op=ALU.mult), deps=kxs + [s1])
    cr2 = st("vector", lambda S, e: e.tensor_scalar(
        out=S.xr[0][0:1, 64:96], in0=S.xr[0][0:1, 64:96],
        scalar1=-1.02 * SC1, scalar2=None, op0=ALU.mult))
    s3 = st("sync", lambda S, e: e.dma_start(
        out=S.rt[7:8, 0:32], in_=S.xr[0][0:1, 64:96]), deps=[cr2], dma=True)
    st("gpsimd", lambda S, e: e.dma_start(
        out=S.c1neg[:, 0:32], in_=S.rt[7:8, 0:32].broadcast_to([128, 32])),
        deps=[s3], dma=True)
    return sc



def make_attn1(S):
    sc = Script(S, "at1")
    st = sc.step
    qt = S.xn[:, 0:512]
    kb = [S.xn[:, 512 + i * 128:512 + (i + 1) * 128] for i in range(2)]
    va = [S.xn[:, 768 + i * 65:768 + (i + 1) * 65] for i in range(2)]
    vbb = [S.xn[:, 898 + i * 64:898 + (i + 1) * 64] for i in range(2)]
    Pb = [S.xn[:, 1026 + i * 512:1026 + (i + 1) * 512] for i in range(2)]
    osA = S.xn[0:64, 2050:2562]
    osB = S.xn[0:64, 2562:3074]
    iotar = S.sb_at("a1_iotar", [1, 512], 0, F32)
    kpcs = S.sb_at("a1_kpcs", [128, 8], 2048, F32)
    qposr = S.sb_at("a1_qposr", [1, 512], 2304, F32)
    pS = [S.p512[0], S.p512[1]]
    qpbc = S.p512[2]
    qpbc2 = S.p512[3]
    pOA = S.pbig[0][0:65, :]
    pOB = S.pbig[1][0:64, :]
    g_io = st("gpsimd", lambda S, e: e.iota(
        iotar[:], pattern=[[1, 512]], base=0, channel_multiplier=0,
        allow_small_or_imprecise_dtypes=True))
    lastq = None
    lastk = [None, None]
    lastva = [None, None]
    lastvb = [None, None]
    lastP = [None, None]
    lastex = [None, None]
    last_mk = None
    last_qbc2 = None
    rt_r3 = [None] * 4
    last_r2 = None
    lastoA = lastoB = None
    normA = normB = None
    prev_r4 = None
    rts = 0
    for qb in range(2):
        kpt = []
        for rk in range(8):
            k1tok = st("vector", lambda S, e, rk=rk, qb=qb: e.tensor_scalar(
                out=kpcs[:, rk:rk + 1], in0=S.posc[:, 4 * qb:4 * qb + 1],
                scalar1=float(128 * rk - 512), scalar2=None, op0=ALU.add))
            if rk < 4:
                k1tok = st("vector", lambda S, e, rk=rk, qb=qb: e.tensor_single_scalar(
                    kpcs[:, rk:rk + 1], kpcs[:, rk:rk + 1],
                    S.pbr[:, 8 + qb:9 + qb], ALU.add))
            kpt.append(k1tok)
        qp = st("vector", lambda S, e, qb=qb: e.tensor_single_scalar(
            qposr[:], iotar[:], S.pbr[0:1, 4 * qb:4 * qb + 1], ALU.add),
            deps=[g_io, last_qbc2])
        qbc = st("tensor", lambda S, e: e.matmul(
            qpbc[:], S.ones_r32[:], qposr[:], start=True, stop=True),
            deps=[qp, last_mk])
        qp2 = st("vector", lambda S, e: e.tensor_scalar(
            out=qposr[:], in0=qposr[:], scalar1=-512.0, scalar2=None,
            op0=ALU.add), deps=[qbc])
        qbc2 = st("tensor", lambda S, e: e.matmul(
            qpbc2[:], S.ones_r32[:], qposr[:], start=True, stop=True),
            deps=[qp2, last_mk])
        last_qbc2 = qbc2
        for g in range(NKV1):
            for h2 in range(4):
                h = 4 * g + h2
                qld = st("sync", lambda S, e, h=h, qb=qb: e.dma_start(
                    out=qt[:], in_=S.q1d[h, :, qb * 512:(qb + 1) * 512]),
                    deps=[lastq], dma=True)
                for rk in range(8):
                    sl = rk % 2
                    mp = rk * 128 if rk < 4 else (rk - 4) * 128
                    if rk >= 4:
                        kld = st("sync", lambda S, e, g=g, qb=qb, mp=mp, sl=sl:
                                 e.dma_start(out=kb[sl][:],
                                             in_=S.k1c[qb, g, :, mp:mp + 128]),
                                 deps=[lastk[sl]], dma=True)
                        vld = st("sync", lambda S, e, g=g, qb=qb, mp=mp, sl=sl:
                                 e.dma_start(out=va[sl][:],
                                             in_=S.v1c[qb, mp:mp + 128,
                                                       129 * g:129 * g + 65]),
                                 deps=[lastva[sl]], dma=True)
                        vld2 = st("sync", lambda S, e, g=g, qb=qb, mp=mp, sl=sl:
                                  e.dma_start(out=vbb[sl][:],
                                              in_=S.v1c[qb, mp:mp + 128,
                                                        129 * g + 65:129 * g + 129]),
                                  deps=[lastvb[sl]], dma=True)
                    else:
                        kld = st("sync", lambda S, e, g=g, qb=qb, mp=mp, sl=sl:
                                 e.dma_start(out=kb[sl][:],
                                             in_=S.k1p[qb, g, :, mp:mp + 128]),
                                 deps=[lastk[sl]], dma=True)
                        vld = st("sync", lambda S, e, g=g, qb=qb, mp=mp, sl=sl:
                                 e.dma_start(out=va[sl][:],
                                             in_=S.v1p[qb, mp:mp + 128,
                                                       129 * g:129 * g + 65]),
                                 deps=[lastva[sl]], dma=True)
                        vld2 = st("sync", lambda S, e, g=g, qb=qb, mp=mp, sl=sl:
                                  e.dma_start(out=vbb[sl][:],
                                              in_=S.v1p[qb, mp:mp + 128,
                                                        129 * g + 65:129 * g + 129]),
                                  deps=[lastvb[sl]], dma=True)
                    ss = st("tensor", lambda S, e, sl=sl: e.matmul(
                        pS[sl][:], kb[sl][:], qt[:], start=True, stop=True),
                        deps=[kld, qld, lastex[sl]])
                    lastk[sl] = ss
                    lastq = ss
                    ex = st("scalar", lambda S, e, sl=sl, h=h: e.activation(
                        out=Pb[sl][:], in_=pS[sl][:], func=AF.Exp, scale=SC1,
                        bias=S.c1neg[:, h:h + 1]), deps=[ss, lastP[sl]])
                    lastex[sl] = ex
                    st("vector", lambda S, e, sl=sl, rk=rk: e.scalar_tensor_tensor(
                        out=Pb[sl][:], in0=qpbc[:], scalar=kpcs[:, rk:rk + 1],
                        in1=Pb[sl][:], op0=ALU.is_ge, op1=ALU.mult),
                        deps=[ex, qbc, kpt[rk]])
                    mk2 = st("vector", lambda S, e, sl=sl, rk=rk: e.scalar_tensor_tensor(
                        out=Pb[sl][:], in0=qpbc2[:], scalar=kpcs[:, rk:rk + 1],
                        in1=Pb[sl][:], op0=ALU.is_lt, op1=ALU.mult),
                        deps=[qbc2])
                    last_mk = mk2
                    pdeA = [vld, mk2]
                    pdeB = [vld2]
                    if rk == 0:
                        pdeA += [normA, normB]
                        pdeB.append(normB)
                    pvA = st("tensor", lambda S, e, sl=sl, rk=rk: e.matmul(
                        pOA, va[sl][:], Pb[sl][:],
                        start=(rk == 0), stop=(rk == 7)), deps=pdeA)
                    lastva[sl] = pvA
                    pvB = st("tensor", lambda S, e, sl=sl, rk=rk: e.matmul(
                        pOB, vbb[sl][:], Pb[sl][:],
                        start=(rk == 0), stop=(rk == 7)), deps=pdeB)
                    lastvb[sl] = pvB
                    lastP[sl] = pvB
                r1 = st("vector", lambda S, e: e.reciprocal(
                    out=S.xr[0][64:65, :], in_=pOA[64:65, :]),
                    deps=[lastP[7 % 2], last_r2])
                slo = rts % 4
                rts += 1
                r2 = st("sync", lambda S, e, slo=slo: e.dma_start(
                    out=S.rt[slo:slo + 1, :], in_=S.xr[0][64:65, :]),
                    deps=[r1, rt_r3[slo]], dma=True)
                last_r2 = r2
                r3 = st("gpsimd", lambda S, e, slo=slo: e.dma_start(
                    out=S.xr[1][:], in_=S.rt[slo:slo + 1, :].broadcast_to([128, 512])),
                    deps=[r2, prev_r4], dma=True)
                rt_r3[slo] = r3
                r4a = st("vector", lambda S, e: e.tensor_tensor(
                    out=osA[:], in0=pOA[0:64, :], in1=S.xr[1][0:64, :],
                    op=ALU.mult), deps=[r3, lastoA])
                r4b = st("vector", lambda S, e: e.tensor_tensor(
                    out=osB[:], in0=pOB[0:64, :], in1=S.xr[1][0:64, :],
                    op=ALU.mult), deps=[lastoB])
                normA, normB = r4a, r4b
                prev_r4 = r4b
                lastoA = st("sync", lambda S, e, h=h, qb=qb: e.dma_start(
                    out=S.ao1[h, 0:64, qb * 512:(qb + 1) * 512], in_=osA[:]),
                    deps=[r4a], dma=True)
                lastoB = st("sync", lambda S, e, h=h, qb=qb: e.dma_start(
                    out=S.ao1[h, 64:128, qb * 512:(qb + 1) * 512], in_=osB[:]),
                    deps=[r4b], dma=True)
    return sc



def make_dequant(S):
    sc = Script(S, "dq")
    st = sc.step
    NT = 128
    OGT = 2048
    crep = [S.xn[:, i * OGT:(i + 1) * OGT] for i in range(2)]
    oht = [S.xn[:, (2 + i) * OGT:(3 + i) * OGT] for i in range(4)]
    stg = [S.xn[:, 6 * OGT + i * 512:6 * OGT + (i + 1) * 512] for i in range(2)]
    cbb = S.sb_at("dq_cbb", [128, 32], 12288, BF16)
    iot2 = S.sb_at("dq_iot2", [128, 2], 12352, F32)
    ps = [S.p512[0], S.p512[1]]
    st("gpsimd", lambda S, e: e.iota(iot2[:, 0:1], pattern=[[0, 1]], base=0,
                                     channel_multiplier=1,
                                     allow_small_or_imprecise_dtypes=True))
    g_io = st("gpsimd", lambda S, e: e.iota(iot2[:, 1:2], pattern=[[0, 1]], base=128,
                                            channel_multiplier=1,
                                            allow_small_or_imprecise_dtypes=True))
    cbl = []
    for cb in range(2):
        for ec in range(2):
            cbl.append(st("gpsimd", lambda S, e, cb=cb, ec=ec: e.dma_start(
                out=cbb[:, (2 * cb + ec) * 8:(2 * cb + ec) * 8 + 8],
                in_=S.cbflat[256 * cb + 128 * ec:256 * cb + 128 * ec + 128, :]),
                dma=True))
    gcast = st("gpsimd", lambda S, e: e.dma_start(
        out=S.codesb[:], in_=S.codes[:]), dma=True)
    prev_mm = {}
    prev_store = {}
    for t in range(NT):
        reps, ohs = [], {}
        for cb in range(2):
            reps.append(st(("sync", "scalar")[cb], lambda S, e, cb=cb, t=t: e.dma_start(
                out=crep[cb][:],
                in_=S.codesb[cb:cb + 1, t * OGT:(t + 1) * OGT]
                .broadcast_to([128, OGT])),
                deps=[prev_mm.get(t - 1), gcast], dma=True))
        for cb in range(2):
            for ec in range(2):
                ohs[(cb, ec)] = st("vector", lambda S, e, cb=cb, ec=ec:
                                   e.tensor_single_scalar(
                                       oht[2 * cb + ec][:], crep[cb][:],
                                       iot2[:, ec:ec + 1], ALU.is_equal),
                                   deps=[reps[cb], prev_mm.get(t - 1), g_io])
        for srt in range(4):
            pss = ps[(t * 4 + srt) % 2]
            mm = None
            for ci, (cb, ec) in enumerate([(0, 0), (0, 1), (1, 0), (1, 1)]):
                mm = st("tensor", lambda S, e, cb=cb, ec=ec, srt=srt, pss=pss, ci=ci:
                        e.matmul(pss[0:8, :],
                                 cbb[:, (2 * cb + ec) * 8:(2 * cb + ec) * 8 + 8],
                                 oht[2 * cb + ec][:, srt * 512:(srt + 1) * 512],
                                 start=(ci == 0), stop=(ci == 3)),
                        deps=[ohs[(cb, ec)], prev_store.get(t * 4 + srt - 2)] + cbl)
            prev_mm[t] = mm
            ev = st("vector", lambda S, e, srt=srt, pss=pss, t=t: e.tensor_copy(
                stg[(t * 4 + srt) % 2][0:8, :], pss[0:8, :]), deps=[mm])
            gval = 4 * t + srt
            prev_store[t * 4 + srt] = st("sync", lambda S, e, gval=gval, srt=srt, t=t:
                                         e.dma_start(
                                             out=S.wqc[8 * gval:8 * gval + 8, :],
                                             in_=stg[(t * 4 + srt) % 2][0:8, :]),
                                         deps=[ev], dma=True)
    st("gpsimd", lambda S, e: e.collective_compute(
        "AllGather", ALU.bypass, replica_groups=[list(range(8))],
        ins=[S.wqc[:]], outs=[S.wqg[:]]),
        deps=[prev_store[(NT - 1) * 4 + 3 - k] for k in range(8)], coll=True)
    return sc


def make_outstore(S):
    sc = Script(S, "out")
    st = sc.step
    toks = []
    for c in range(NRES):
        toks.append(st("sync", lambda S, e, c=c: e.dma_start(
            out=S.out[c * 128:(c + 1) * 128, :], in_=S.xs[:, c * TOK:(c + 1) * TOK]),
            dma=True))
    for c in range(NRES, NCH):
        toks.append(st("sync", lambda S, e, c=c: e.dma_start(
            out=S.out[c * 128:(c + 1) * 128, :], in_=S.xpark[c - NRES]),
            dma=True))
    st("sync", lambda S, e: e.nop(), deps=toks)
    return sc


def _add_casts(sc, casts):
    for src, dst, n in casts:
        for oc in range(n):
            sc.step("gpsimd", lambda S, e, src=src, dst=dst, oc=oc: e.dma_start(
                out=dst[oc], in_=src[oc]), dma=True)
    return sc


def build_phases(S, upto=None, skip=()):
    wq_slabs = [S.wqg[oc // 4, :, (oc % 4) * 128:(oc % 4) * 128 + 128]
                .rearrange("(c p) o -> p c o", p=128) for oc in range(32)]
    mk_cv1 = lambda: make_conv(S, [
        (lambda oc: S.qc1[oc], 8, 1, lambda S_, c, oc: S_.q0[:, oc, :], q1dst,
         lambda S_, oc: S_.cbq1[:, oc:oc + 1], lambda b: 516 * b),
        (lambda oc: S.kc1[oc], 2, 1, lambda S_, c, oc: S_.k0[:, oc, :], k1dst,
         lambda S_, oc: S_.cbk1[:, oc:oc + 1], lambda b: 516 * b),
    ], pre2=True, tag="cv1")
    mk_cv2 = lambda: make_conv(S, [
        (lambda oc: S.qc2T[oc], 8, 8, lambda S_, c, oc: S_.q1[:, c, :], q2dst,
         lambda S_, oc: S_.cbq2[:, oc:oc + 1], lambda b: 514 * b),
        (lambda oc: S.kc2T[oc], 2, 2, lambda S_, c, oc: S_.k1[:, c, :], k2dst,
         lambda S_, oc: S_.cbk2[:, oc:oc + 1], lambda b: 514 * b),
    ], pre2=False, tag="cv2")
    phases = [
        ("p0", lambda: make_p0(S)),
        ("dq", lambda: make_dequant(S)),
        ("lns1", lambda: make_lnstats(S, "lns1", "ln")),
        ("qkv", lambda: make_proj(S, [([S.qkvT[oc] for oc in range(12)], 12,
                                       evac_qkv())], tag="qkv")),
        ("halo", lambda: make_halocc(S)),
        ("cv1", mk_cv1),
        ("cv2", mk_cv2),
        ("mf", lambda: make_meanfix(S)),
        ("pp", lambda: make_postproc(S)),
        ("cc1", lambda: make_cc1(S)),
        ("at0", lambda: make_attn0(S)),
        ("wo0", lambda: make_ao_proj(S, S.woT, 32, 8, S.ao0, tag="wo0")),
        ("lns2", lambda: make_lnstats(S, "lns2", "ln+rms")),
        ("ffn0", lambda: make_proj(
            S, [([S.ffn0T[oc] for oc in range(32)], 32,
                 evac_residual(scale=lambda S_, oc: S_.gcols[:, oc:oc + 1]))],
            tag="ffn0", post_nw=True)),
        ("lns3", lambda: make_lnstats(S, "lns3", "ln")),
        ("l1p", lambda: make_l1proj(S)),
        ("v1b", lambda: make_v1build(S)),
        ("cc2", lambda: make_cc2(S)),
        ("at1", lambda: make_attn1(S)),
        ("wo1", lambda: make_ao_proj(S, S.l1oT, 32, 32, S.ao1, tag="wo1")),
        ("lns4", lambda: make_lnstats(S, "lns4", "ln")),
        ("ffn1", lambda: make_proj(S, [(wq_slabs, 32,
                                        evac_residual(scale=lambda S_, oc:
                                                      S_.scol[:, oc:oc + 1],
                                                      bias=lambda S_, oc:
                                                      S_.bcol[:, oc:oc + 1]))],
                                   tag="ffn1", weng="scalar")),
        ("out", lambda: make_outstore(S)),
    ]
    out = []
    for name, mk in phases:
        if name in skip:
            continue
        sc = mk()
        sc.name = name
        sc.snap_drain()
        out.append(sc)
        if upto is not None and name == upto:
            break
    return out



def _slabs(wT, n_oc, kdim):
    nc_ = kdim // 128
    return np.ascontiguousarray(
        wT.reshape(nc_, 128, n_oc, 128).transpose(2, 1, 0, 3)
        .reshape(n_oc, 128, nc_ * 128).astype(np.float32))


def _pack_conv1(w, n_oc):
    out = np.zeros((n_oc, 128, 3, 128), dtype=np.float32)
    for oc in range(n_oc):
        for o2 in range(128):
            och = 128 * oc + o2
            hh = o2 // 64
            for tau in range(3):
                out[oc, hh * 64:(hh + 1) * 64, tau, o2] = w[och, :, tau]
    return np.ascontiguousarray(out.reshape(n_oc, 128, 384))


def _pack_conv2(w, n_oc, kch):
    out = np.zeros((n_oc, 128, 3, kch, 128), dtype=np.float32)
    for oc in range(n_oc):
        blk = w[oc * 128:(oc + 1) * 128]
        for tau in range(3):
            for ic in range(kch):
                out[oc, :, tau, ic, :] = blk[:, ic * 128:(ic + 1) * 128, tau].T
    return np.ascontiguousarray(out.reshape(n_oc, 128, 3 * kch * 128))


def _cols(v):
    return np.ascontiguousarray(v.reshape(32, 128).T.astype(np.float32))


def make_in_maps(inp):
    xt = np.ascontiguousarray(inp["L0_attn_x"][0].T.astype(np.float32))
    wqkvT = _slabs(np.ascontiguousarray(inp["L0_attn_w_qkv"].T), 12, D)
    qc1 = _pack_conv1(inp["L0_attn_qc1_w"], 8)
    qc2 = _pack_conv2(inp["L0_attn_qc2_w"], 8, 8)
    kc1 = _pack_conv1(inp["L0_attn_kc1_w"], 2)
    kc2 = _pack_conv2(inp["L0_attn_kc2_w"], 2, 2)
    cbq1 = np.ascontiguousarray(inp["L0_attn_qc1_b"].reshape(8, 128).T.astype(np.float32))
    cbq2 = np.ascontiguousarray(inp["L0_attn_qc2_b"].reshape(8, 128).T.astype(np.float32))
    cbk1 = np.ascontiguousarray(inp["L0_attn_kc1_b"].reshape(2, 128).T.astype(np.float32))
    cbk2 = np.ascontiguousarray(inp["L0_attn_kc2_b"].reshape(2, 128).T.astype(np.float32))
    ktv = inp["L0_attn_key_temp"].reshape(1, 1).astype(np.float32)
    woT = _slabs(np.ascontiguousarray(inp["L0_attn_w_o"].T), 32, 1024)
    ffn0T = _slabs(np.ascontiguousarray(inp["L0_ffn_w_q"].T), 32, D)
    nwc = _cols(inp["L0_ffn_norm_weight"])
    gam = _cols(inp["L0_ffn_gamma"])
    l1qT = _slabs(np.ascontiguousarray(inp["L1_attn_wq"].T), 32, D)
    l1kT = _slabs(np.ascontiguousarray(inp["L1_attn_wk"].T), 8, D)
    l1vT = _slabs(np.ascontiguousarray(inp["L1_attn_wv"].T), 8, D)
    l1oT = _slabs(np.ascontiguousarray(inp["L1_attn_wo"].T), 32, D)
    cbf = np.ascontiguousarray(
        inp["L1_ffn_codebooks"].reshape(512, 8).astype(np.float32))
    scl = _cols(inp["L1_ffn_scales"].reshape(4096))
    fb = _cols(inp["L1_ffn_bias"])

    in_maps = []
    for core in range(8):
        bA, bB = core, 15 - core
        xc = np.ascontiguousarray(np.concatenate(
            [xt[:, bA * TB:(bA + 1) * TB], xt[:, bB * TB:(bB + 1) * TB]], axis=1))
        codes_sl = inp["L1_ffn_codes"][core * 512:(core + 1) * 512]
        codes_rows = np.ascontiguousarray(
            codes_sl.transpose(2, 1, 0).reshape(2, 512 * 512).astype(np.int32))
        pb = np.zeros((1, 16), dtype=np.float32)
        for t in range(8):
            pb[0, t] = (bA if t < 4 else bB) * TB + (t % 4) * 128
        for qb, blk in enumerate((bA, bB)):
            if blk == 0:
                pb[0, 8 + qb] = 1e9
        prm = np.zeros((1, 16), dtype=np.int32)
        for qb, blk in enumerate((bA, bB)):
            if blk > 0:
                r, sl = blk_owner(blk - 1)
                prm[0, qb] = (r * 2 + sl) * 12
                prm[0, 2 + qb] = r * 2 + sl
                for mm4 in range(4):
                    prm[0, 4 + 4 * qb + mm4] = (r * 2 + sl) * 512 + mm4 * 128
        hz = np.ones((128, 2), dtype=np.float32)
        if bA == 0:
            hz[:, 0] = 0.0
        in_maps.append({
            "x": xc, "qkvT": wqkvT, "qc1": qc1, "qc2T": qc2, "kc1": kc1,
            "kc2T": kc2, "cb_q1": cbq1, "cb_q2": cbq2, "cb_k1": cbk1,
            "cb_k2": cbk2, "key_temp": ktv, "woT": woT, "ffn0T": ffn0T,
            "nwc": nwc, "gamma": gam, "l1qT": l1qT, "l1kT": l1kT, "l1vT": l1vT,
            "l1oT": l1oT, "codes": codes_rows, "cbflat": cbf, "scales": scl,
            "fbias": fb, "pbase": pb, "prm": prm, "hz": hz,
        })
    return in_maps


_CACHE = {}


def kernel(**inputs):
    inp = {k: np.asarray(v) for k, v in inputs.items()}
    in_maps = make_in_maps(inp)
    if "nc" not in _CACHE:
        S = build_state(0)
        phases = build_phases(S)
        _CACHE["nc"] = finalize(S, phases)
    nc = _CACHE["nc"]
    res = run_bass_kernel_spmd(nc, in_maps, core_ids=list(range(8)))
    out = np.zeros((1, 8192, D), dtype=np.float32)
    for core in range(8):
        o = np.asarray(res.results[core]["out"])
        bA, bB = core, 15 - core
        out[0, bA * TB:(bA + 1) * TB, :] = o[:, 0:TB].T
        out[0, bB * TB:(bB + 1) * TB, :] = o[:, TB:2 * TB].T
    return out
